# version 23 (final)
# baseline (speedup 1.0000x reference)
import numpy as np
import ml_dtypes
import concourse.bass as bass
import concourse.mybir as mybir
from concourse.bass_utils import run_bass_kernel_spmd
from contextlib import ExitStack

F32 = mybir.dt.float32
BF16 = mybir.dt.bfloat16
I32 = mybir.dt.int32
AF = mybir.ActivationFunctionType
ALU = mybir.AluOpType
AX = mybir.AxisListType

SEQ = 4096
D = 1024
DK = 64
NH = 8
HPQ = 4
NQ = 4
HID = 3072
S_BIT = 16.0
EXP_BIAS = np.float32(-3.0 * S_BIT * S_BIT)
C127 = np.float32(1.0 / 127.0)
C127SQ = float(np.float32(1.0 / (127.0 * 127.0)))
RD = 1.0 / D
F_EIGHTH = 0.125

ENGINES = ("sync", "gpsimd", "tensor", "scalar", "vector")


def _host_consts():
    ident = np.eye(128, dtype=np.float32)
    codes = np.array([[S_BIT if (n >> b) & 1 else -S_BIT for b in range(3)]
                      for n in range(NH)], np.float32)
    cb = np.zeros((96, 12), np.float32)
    for h in range(HPQ):
        for n in range(NH):
            cb[8 * h + n, 3 * h: 3 * h + 3] = codes[n]
    perms = np.zeros((96, 288), np.float32)
    for ri, dd in enumerate((1, 2, 4)):
        for h in range(HPQ):
            for n in range(NH):
                perms[8 * h + n, ri * 96 + 8 * h + (n ^ dd)] = 1.0
    return ident, cb, perms


class Prog:

    def __init__(self, nc, es):
        self.nc = nc
        self.es = es
        self.m = {}
        self.cnt = {}
        self.sems = {}
        for k in ("TE", "SE", "VE"):
            self._mk(k)
        self.seq = {e: [] for e in ENGINES}

    def _mk(self, k):
        if k not in self.sems:
            self.sems[k] = self.es.enter_context(self.nc.semaphore(k))
            self.cnt[k] = 0

    def em(self, eng, fn, waits=(), inc=None, inck=1, key=None):
        if inc is not None:
            self._mk(inc)
            self.cnt[inc] += inck
            if key is not None:
                assert key not in self.m, f"duplicate milestone key {key}"
                self.m[key] = (inc, self.cnt[inc])
        self.seq[eng].append((fn, tuple(waits), inc, inck))

    def barrier(self):
        snap = [(s, v) for s, v in self.cnt.items() if v]
        for e in ENGINES:
            self.seq[e].append((None, tuple(("LIT", s, v) for s, v in snap), None, 1))

    def emit(self, block):
        sems, m = self.sems, self.m
        for e in ENGINES:
            ops = self.seq[e]

            def body(eng, ops=ops):
                for fn, waits, inc, inck in ops:
                    for w in waits:
                        if isinstance(w, tuple) and len(w) == 3 and w[0] == "LIT":
                            eng.wait_ge(sems[w[1]], w[2])
                        else:
                            sname, val = m[w]
                            eng.wait_ge(sems[sname], val)
                    if fn is not None:
                        inst = fn(eng)
                        if inc is not None:
                            inst.then_inc(sems[inc], inck)
            getattr(block, e)(body)


def build_program(debug=False):
    nc = bass.Bass("TRN2", target_bir_lowering=False)
    d_x = nc.dram_tensor("x", [SEQ, D], F32, kind="ExternalInput").ap()
    d_w = {n: nc.dram_tensor(n, [D, D], F32, kind="ExternalInput").ap()
           for n in ("wq", "wk", "wv", "wo")}
    d_b = {n: nc.dram_tensor(n, [D], F32, kind="ExternalInput").ap()
           for n in ("bq", "bk", "bv", "bo")}
    d_r = nc.dram_tensor("r", [NH, DK], F32, kind="ExternalInput").ap()
    d_wg = nc.dram_tensor("wg", [HID, D], I32, kind="ExternalInput").ap()
    d_wu = nc.dram_tensor("wu", [HID, D], I32, kind="ExternalInput").ap()
    d_wd = nc.dram_tensor("wd", [D, HID], I32, kind="ExternalInput").ap()
    d_ident = nc.dram_tensor("identf", [128, 128], F32, kind="ExternalInput").ap()
    d_cb = nc.dram_tensor("cbc", [96, 12], F32, kind="ExternalInput").ap()
    d_perms = nc.dram_tensor("permsc", [96, 288], F32, kind="ExternalInput").ap()
    d_out = nc.dram_tensor("out", [SEQ, D], F32, kind="ExternalOutput").ap()
    _scr = "ExternalOutput" if debug else "Internal"
    d_cat = nc.dram_tensor("catscr", [D, SEQ], BF16, kind=_scr).ap()
    d_wgu = nc.dram_tensor("wguscr", [24 * 128, 2048], BF16, kind=_scr).ap()
    d_wdt = nc.dram_tensor("wdtscr", [24 * 128, 1024], BF16, kind=_scr).ap()
    if debug:
        d_dxnt = nc.dram_tensor("dxnt", [1024, SEQ], BF16, kind="ExternalOutput").ap()
        d_daugk = nc.dram_tensor("daugk", [128, HPQ * SEQ], BF16, kind="ExternalOutput").ap()
        d_dvaug = nc.dram_tensor("dvaug", [128, 32 * HPQ * 65], BF16, kind="ExternalOutput").ap()
        d_daugq = nc.dram_tensor("daugq", [128, HPQ * 512], BF16, kind="ExternalOutput").ap()
        d_dexp = nc.dram_tensor("dexp", [128, 1024], BF16, kind="ExternalOutput").ap()
        d_ddn = nc.dram_tensor("ddn", [1, 512], BF16, kind="ExternalOutput").ap()
        d_dbcs = nc.dram_tensor("dbcs", [64, 512], F32, kind="ExternalOutput").ap()
        d_dpv = nc.dram_tensor("dpv", [64, 512], F32, kind="ExternalOutput").ap()
        d_dstat = nc.dram_tensor("dstat", [128, 64], F32, kind="ExternalOutput").ap()
        d_dxntb = nc.dram_tensor("dxntb", [128, 4096], BF16, kind="ExternalOutput").ap()
        d_dhts = nc.dram_tensor("dhts", [128, 12288], BF16, kind="ExternalOutput").ap()

    DBG_XNTB = [debug]
    es = ExitStack()
    with es:
        ET = es.enter_context
        P = Prog(nc, es)

        PSA = ET(nc.psum_tensor("PSA", [128, 1024], F32))
        PSB = ET(nc.psum_tensor("PSB", [128, 1024], F32))
        PSC = ET(nc.psum_tensor("PSC", [128, 1024], F32))
        PSD = ET(nc.psum_tensor("PSD", [128, 1024], F32))
        SCPS = (PSA, PSB)
        TRB = [(PSC, 0), (PSA, 0), (PSB, 0), (PSC, 512), (PSA, 512), (PSB, 512)]

        IDENTF = ET(nc.sbuf_tensor("identf_s", [128, 128], F32))
        IDENTB = ET(nc.sbuf_tensor("identb_s", [128, 128], BF16))
        CBb = ET(nc.sbuf_tensor("cbb", [96, 12], BF16))
        RS = ET(nc.sbuf_tensor("rs", [128, 8], BF16))
        ONESB = ET(nc.sbuf_tensor("onesb", [1, 512], BF16))
        BQTF = ET(nc.sbuf_tensor("bqtf", [128, 8], F32))
        BKTF = ET(nc.sbuf_tensor("bktf", [128, 8], F32))
        BQTB = ET(nc.sbuf_tensor("bqtb", [128, 8], BF16))
        BKTB = ET(nc.sbuf_tensor("bktb", [128, 8], BF16))
        BOROW = ET(nc.sbuf_tensor("borow", [1, D], BF16))
        BVROW = ET(nc.sbuf_tensor("bvrow", [1, D], BF16))
        LNS = [ET(nc.sbuf_tensor(f"lns{i}", [128, 8], F32)) for i in range(2)]
        SREG = ET(nc.sbuf_tensor("sreg", [128, 64], F32))
        P0T = ET(nc.sbuf_tensor("p0t", [8, 128], F32))
        P0R = ET(nc.sbuf_tensor("p0r", [8, 64], F32))

        xnt_scope = ExitStack()
        XS = xnt_scope.enter_context
        XNT = [ET(nc.sbuf_tensor(f"xnt{c}", [128, SEQ], BF16)) for c in range(8)]
        PERMb = XS(nc.sbuf_tensor("permb", [96, 288], F32))
        ONEHOT = XS(nc.sbuf_tensor("onehot", [96, 512], BF16))
        LSB = XS(nc.sbuf_tensor("lsb", [96, 512], F32))
        LSB2 = XS(nc.sbuf_tensor("lsb2", [96, 512], F32))
        LA = XS(nc.sbuf_tensor("la", [96, 512], F32))
        LB = XS(nc.sbuf_tensor("lb", [96, 512], F32))
        LC = XS(nc.sbuf_tensor("lc", [96, 512], F32))
        OFFQ = XS(nc.sbuf_tensor("offq", [32, 1], F32))
        OFFK = XS(nc.sbuf_tensor("offk", [32, 1], F32))
        EXPS = [XS(nc.sbuf_tensor(f"exps{i}", [128, 1024], BF16)) for i in range(8)]
        DNBF = [XS(nc.sbuf_tensor(f"dnbf{i}", [1, 512], BF16)) for i in range(2)]
        BCS = [XS(nc.sbuf_tensor(f"bcs{i}", [64, 512], F32)) for i in range(2)]
        RCF = [XS(nc.sbuf_tensor(f"rcf{i}", [64, 512], F32)) for i in range(2)]
        CATS = [XS(nc.sbuf_tensor(f"cats{i}", [128, 512], BF16)) for i in range(2)]

        a_scope = ExitStack()
        AS = a_scope.enter_context
        XNF = [AS(nc.sbuf_tensor(f"xnf{i}", [128, 1024], F32)) for i in range(2)]
        SQS = AS(nc.sbuf_tensor("sqs", [128, 1024], F32))
        XT = [AS(nc.sbuf_tensor(f"xt{i}", [128, 1024], F32)) for i in range(3)]

        block = ET(nc.Block())

        P.em("sync", lambda sy: sy.dma_start(IDENTF[:, :], d_ident), inc="Cmisc", inck=16, key="ld_id")
        P.em("sync", lambda sy: sy.dma_start(P0T[:, :], d_b["bq"].rearrange("(t p) -> t p", p=128)),
             inc="Cmisc", inck=16, key="ld_bq")
        P.em("sync", lambda sy: sy.dma_start(P0R[:, :], d_r), inc="Cmisc", inck=16, key="ld_r")
        P.em("gpsimd", lambda gp: gp.dma_start(IDENTB[:, :], d_ident), inc="Gmisc", inck=16,
             key="ld_idb")
        P.em("gpsimd", lambda gp: gp.dma_start(CBb[:, :], d_cb), inc="Gmisc", inck=16)
        P.em("gpsimd", lambda gp: gp.dma_start(PERMb[:, :], d_perms), inc="Gmisc", inck=16)
        P.em("gpsimd", lambda gp: gp.dma_start(BOROW[:, :], d_b["bo"].rearrange("(a e) -> a e", a=1)),
             inc="Gmisc", inck=16)
        P.em("gpsimd", lambda gp: gp.dma_start(BVROW[:, :], d_b["bv"].rearrange("(a e) -> a e", a=1)),
             inc="Gmisc", inck=16, key="ld_bv")
        P.em("tensor", lambda t: t.transpose(PSA[0:128, 0:8], P0T[:, :], IDENTF[0:8, 0:8]),
             waits=("ld_r",), inc="TE", key="t_bq")
        P.em("tensor", lambda t: t.transpose(PSA[0:64, 512:520], P0R[:, :], IDENTF[0:8, 0:8]),
             waits=("ld_r",), inc="TE", key="t_r")
        P.em("vector", lambda v: v.tensor_copy(BQTF[:, :], PSA[0:128, 0:8]), waits=("t_bq",))
        P.em("vector", lambda v: v.tensor_copy(BQTB[:, :], PSA[0:128, 0:8]))
        P.em("vector", lambda v: v.tensor_copy(RS[0:64, :], PSA[0:64, 512:520]), waits=("t_r",))
        P.em("vector", lambda v: v.tensor_copy(RS[64:128, :], PSA[0:64, 512:520]),
             inc="VE", key="v_bq")
        P.em("sync", lambda sy: sy.dma_start(P0T[:, :], d_b["bk"].rearrange("(t p) -> t p", p=128)),
             waits=("v_bq",), inc="Cmisc", inck=16, key="ld_bk")
        P.em("tensor", lambda t: t.transpose(PSB[0:128, 0:8], P0T[:, :], IDENTF[0:8, 0:8]),
             waits=("ld_bk",), inc="TE", key="t_bk")
        P.em("vector", lambda v: v.tensor_copy(BKTF[:, :], PSB[0:128, 0:8]), waits=("t_bk",))
        P.em("vector", lambda v: v.tensor_copy(BKTB[:, :], PSB[0:128, 0:8]))
        P.em("vector", lambda v: v.memset(ONESB[:, :], 1.0))
        P.em("vector", lambda v: v.memset(ONEHOT[:, :], 0.0))
        P.em("vector", lambda v: v.memset(LSB[:, :], 0.0))
        P.em("vector", lambda v: v.memset(LSB2[:, :], 0.0))
        P.em("vector", lambda v: v.memset(LA[:, :], 0.0))
        P.em("vector", lambda v: v.memset(LB[:, :], 0.0))
        P.em("vector", lambda v: v.memset(LC[:, :], 0.0), inc="VE", key="v_c0")
        P.barrier()

        def ln_stats(tag, it, par, xsrc):
            L = LNS[par]
            P.em("vector", lambda v, x=xsrc, L=L: v.reduce_sum(L[:, 0:1], x, AX.X),
                waits=((tag, "xload", it),), inc="VE", key=(tag, "red", it))
            P.em("vector", lambda v, L=L: v.tensor_scalar_mul(L[:, 1:2], L[:, 0:1], RD),
                waits=((tag, "red", it),), inc="VE", key=(tag, "mean", it))
            P.em("vector", lambda v, L=L: v.tensor_mul(L[:, 2:3], L[:, 1:2], L[:, 1:2]),
                waits=((tag, "mean", it),), inc="VE", key=(tag, "m2", it))
            P.em("vector", lambda v, x=xsrc: v.tensor_mul(SQS[:, :], x, x),
                 waits=((tag, "xload", it),), inc="VE", key=(tag, "sqm", it))
            P.em("vector", lambda v, L=L: v.reduce_sum(L[:, 6:7], SQS[:, :], AX.X),
                 waits=((tag, "sqm", it),), inc="VE", key=(tag, "sq", it))
            P.em("vector", lambda v, L=L: v.tensor_scalar(
                L[:, 3:4], L[:, 6:7], RD, L[:, 2:3], ALU.mult, ALU.subtract),
                waits=((tag, "sq", it), (tag, "m2", it)), inc="VE", key=(tag, "var", it))
            P.em("scalar", lambda s, L=L: s.activation(
                L[:, 4:5], L[:, 3:4], AF.Ln, bias=np.float32(1e-5)),
                waits=((tag, "var", it),), inc="SE", key=(tag, "lnv", it))
            P.em("scalar", lambda s, L=L: s.activation(
                L[:, 5:6], L[:, 4:5], AF.Exp, scale=np.float32(-0.5)),
                waits=((tag, "lnv", it),), inc="SE", key=(tag, "sqa", it))

        def ln_apply(tag, it, par, xsrc, lpar):
            L = LNS[lpar]
            P.em("vector", lambda v, L=L, x=xsrc, par=par: v.tensor_scalar(
                XNF[par][:, :], x, L[:, 1:2], L[:, 5:6], ALU.subtract, ALU.mult),
                waits=((tag, "sqa", it),), inc="VE", key=(tag, "apply", it))

        def transpose_tile(tag, it, par, dst_fn, k):
            for cc in range(8):
                ts = k + cc
                pt, po = TRB[ts % 6]
                w = [(tag, "apply", it)] if cc == 0 else []
                if ts >= 6:
                    w.append((tag, "xevac", ts - 6))
                P.em("tensor", lambda t, par=par, cc=cc, pt=pt, po=po: t.transpose(
                    pt[:, po:po + 128],
                    XNF[par][:, cc * 128:(cc + 1) * 128], IDENTF[:, :]),
                    waits=w, inc="TE", key=(tag, "trans", ts))
                P.em("vector", lambda v, cc=cc, dst=dst_fn(cc), pt=pt, po=po: v.tensor_copy(
                    dst, pt[:, po:po + 128]),
                    waits=((tag, "trans", ts),), inc="VE", key=(tag, "xevac", ts))

        for it in range(33):
            if it < 32:
                p3 = it % 3
                w = [("A", "apply", it - 3)] if it >= 3 else []
                P.em("sync", lambda sy, it=it, p3=p3: sy.dma_start(
                    XT[p3][:, :], d_x[it * 128:(it + 1) * 128, :]),
                    waits=w, inc=f"CxtA{p3}", inck=16, key=("A", "xload", it))
                ln_stats("A", it, it % 2, XT[p3][:, :])
            if it >= 1:
                jt = it - 1
                ln_apply("A", jt, jt % 2, XT[jt % 3][:, :], jt % 2)
                P.em("vector", lambda v, jt=jt, lp=jt % 2: (
                    v.tensor_copy(SREG[:, jt * 2:jt * 2 + 1], LNS[lp][:, 1:2]),
                    v.tensor_copy(SREG[:, jt * 2 + 1:jt * 2 + 2], LNS[lp][:, 5:6]))[-1],
                    waits=(("A", "apply", jt),), inc="VE", key=("sreg", jt))
                transpose_tile("A", jt, jt % 2,
                               lambda cc, jt=jt: XNT[cc][:, jt * 128:(jt + 1) * 128], jt * 8)
        P.barrier()
        a_scope.close()
        if debug:
            for cc in range(8):
                P.em("sync", lambda sy, cc=cc: sy.dma_start(
                    d_dxnt[cc * 128:(cc + 1) * 128, :], XNT[cc][:, :]), inc="Cdbg", inck=16)
            P.barrier()

        for g in range(NQ):
            d0 = g * 256
            qs = ExitStack()
            with qs:
                QT = qs.enter_context
                AUGK = QT(nc.sbuf_tensor(f"augk{g}", [128, HPQ * SEQ], BF16))
                VAUG = QT(nc.sbuf_tensor(f"vaug{g}", [128, 32 * HPQ * 65], BF16))
                AUGQ = QT(nc.sbuf_tensor(f"augq{g}", [128, HPQ * 512], BF16))
                WQH = QT(nc.sbuf_tensor(f"wqh{g}", [128, 2048], BF16))
                WRQ = QT(nc.sbuf_tensor(f"wrq{g}", [128, 256], BF16))
                WRK = QT(nc.sbuf_tensor(f"wrk{g}", [128, 256], BF16))
                WKH = QT(nc.sbuf_tensor(f"wkh{g}", [128, 2048], BF16))
                WVH = QT(nc.sbuf_tensor(f"wvh{g}", [128, 2048], BF16))
                WTFB = QT(nc.sbuf_tensor(f"wtfb{g}", [128, 2048], F32))
                WT2 = QT(nc.sbuf_tensor(f"wt2{g}", [128, 2048], BF16))
                BVS = QT(nc.sbuf_tensor(f"bvs{g}", [128, 256], F32))

                P.em("gpsimd", lambda gp, WKH=WKH, d0=d0: gp.dma_start(
                    WKH[:, :].rearrange("p (cc d) -> p cc d", d=256),
                    d_w["wk"][:, d0:d0 + 256].rearrange("(cc p) d -> p cc d", p=128)),
                    inc="Gwk", inck=16, key=("wkh", g))
                P.em("gpsimd", lambda gp, d0=d0, WVH=WVH: gp.dma_start(
                    WVH[:, :].rearrange("p (cc d) -> p cc d", d=256),
                    d_w["wv"][:, d0:d0 + 256].rearrange("(cc p) d -> p cc d", p=128)),
                    inc="Gwv", inck=16, key=("wvh", g))
                P.em("gpsimd", lambda gp, d0=d0, WQH=WQH: gp.dma_start(
                    WQH[:, :].rearrange("p (cc d) -> p cc d", d=256),
                    d_w["wq"][:, d0:d0 + 256].rearrange("(cc p) d -> p cc d", p=128)),
                    inc="Gwq", inck=16, key=("wqh", g))
                P.em("vector", lambda v, AUGK=AUGK: v.memset(AUGK[64:128, :], 0.0),
                     inc="VE", key=("akz", g))
                P.em("vector", lambda v, AUGQ=AUGQ: v.memset(AUGQ[64:128, :], 0.0))
                P.em("vector", lambda v, VAUG=VAUG: v.memset(VAUG[:, :], 1.0),
                     inc="VE", key=("vz", g))

                for side, (wsrc, WRdst, OFFdst, bcol) in enumerate(
                        ((d_w["wk"], WRK, OFFK, BKTB), (d_w["wq"], WRQ, OFFQ, BQTB))):
                    P.em("sync", lambda sy, wsrc=wsrc, d0=d0, WTFB=WTFB: sy.dma_start(
                        WTFB[:, :].rearrange("p (c n) -> p c n", n=256),
                        wsrc[:, d0:d0 + 256].rearrange("(c p) n -> p c n", p=128)),
                        waits=[("wtt", g, 0, 7, 1)] if side == 1 else [],
                        inc=f"Cwtfb{side}", inck=16, key=("wtl2", g, side))
                    for cc in range(8):
                        kl = side * 8 + cc
                        for db in range(2):
                            kser = kl * 2 + db
                            pt, po = TRB[kser % 6]
                            w2 = [("wtl2", g, side)] if cc == 0 and db == 0 else []
                            if kser >= 6:
                                pk = kser - 6
                                w2.append(("wte", g, pk // 16, (pk % 16) // 2, pk % 2))
                            P.em("tensor", lambda t, cc=cc, db=db, pt=pt, po=po,
                                 WTFB=WTFB: t.transpose(
                                    pt[:, po:po + 128],
                                    WTFB[:, cc * 256 + db * 128:cc * 256 + (db + 1) * 128],
                                    IDENTF[:, :]),
                                waits=w2, inc="TE", key=("wtt", g, side, cc, db))
                            P.em("vector", lambda v, cc=cc, db=db, pt=pt, po=po,
                                 WT2=WT2: v.tensor_copy(
                                    WT2[:, db * 1024 + cc * 128: db * 1024 + (cc + 1) * 128],
                                    pt[:, po:po + 128]),
                                waits=(("wtt", g, side, cc, db),), inc="VE",
                                key=("wte", g, side, cc, db))
                    for cc in range(8):
                        pb = (cc % 2) * 512
                        for hl in range(HPQ):
                            w3 = []
                            if cc == 0 and hl == 0:
                                w3 = [("wte", g, side, 7, 1), "v_bq"]
                            elif hl == 0 and cc >= 2:
                                w3 = [("wre", g, side, cc - 2)]
                            P.em("tensor", lambda t, cc=cc, hl=hl, pb=pb, WT2=WT2: t.matmul(
                                PSD[0:128, pb + hl * 8:pb + (hl + 1) * 8],
                                WT2[(hl % 2) * 64:(hl % 2) * 64 + 64,
                                    (hl // 2) * 1024 + cc * 128:(hl // 2) * 1024 + (cc + 1) * 128],
                                RS[(hl % 2) * 64:(hl % 2) * 64 + 64, 0:8],
                                start=True, stop=True),
                                waits=w3, inc="TE", key=("wrm", g, side, cc, hl))
                        P.em("vector", lambda v, cc=cc, pb=pb, WRdst=WRdst: v.tensor_copy(
                            WRdst[:, cc * 32:(cc + 1) * 32], PSD[0:128, pb:pb + 32]),
                            waits=(("wrm", g, side, cc, HPQ - 1),), inc="VE",
                            key=("wre", g, side, cc))
                    for cc in range(8):
                        P.em("tensor", lambda t, cc=cc, WRdst=WRdst, bcol=bcol: t.matmul(
                            PSD[0:32, 40:41], WRdst[:, cc * 32:(cc + 1) * 32],
                            bcol[:, cc:cc + 1], start=(cc == 0), stop=(cc == 7)),
                            waits=(("wre", g, side, 7),) if cc == 0 else (),
                            inc="TE" if cc == 7 else None, key=("offm", g, side))
                    P.em("vector", lambda v, OFFdst=OFFdst: v.tensor_copy(
                        OFFdst[:, :], PSD[0:32, 40:41]),
                        waits=(("offm", g, side),), inc="VE", key=("off", g, side))

                KSL = [(PSA, 0), (PSB, 0), (PSD, 0), (PSD, 512)]
                for jt in range(8):
                    for p in range(2):
                        kk = jt * 2 + p
                        kt, ko = KSL[kk % 4]
                        w = [("wkh", g), ("wte", g, 1, 7, 1), ("off", g, 1)] if kk == 0 else []
                        if kk >= 4:
                            w.append(("kev", g, kk - 4, 1))
                        for cc in range(8):
                            P.em("tensor", lambda t, cc=cc, p=p, jt=jt, kk=kk, kt=kt, ko=ko,
                                 WKH=WKH: t.matmul(
                                    kt[:, ko:ko + 512],
                                    WKH[:, cc * 256 + p * 128:cc * 256 + (p + 1) * 128],
                                    XNT[cc][:, jt * 512:(jt + 1) * 512],
                                    start=(cc == 0), stop=(cc == 7)),
                                waits=w if cc == 0 else (), inc="TE" if cc == 7 else None,
                                key=("kpm", g, kk))
                        for eo in range(2):
                            hl = 2 * p + eo
                            P.em("vector", lambda v, hl=hl, eo=eo, p=p, jt=jt, kk=kk,
                                 kt=kt, ko=ko, g=g, AUGK=AUGK: v.tensor_scalar(
                                     AUGK[0:64, hl * SEQ + jt * 512: hl * SEQ + (jt + 1) * 512],
                                     kt[eo * 64:eo * 64 + 64, ko:ko + 512],
                                     BKTF[eo * 64:eo * 64 + 64, 2 * g + p:2 * g + p + 1],
                                     F_EIGHTH, ALU.add, ALU.mult),
                                 waits=(("kpm", g, kk), ("akz", g)) if eo == 0 else (("kpm", g, kk),),
                                 inc="VE", key=("kev", g, kk, eo))

                P.em("tensor", lambda t, d0=d0: t.matmul(
                    PSD[0:128, 512:768], ONESB[0:1, 0:128],
                    BVROW[0:1, d0:d0 + 256], start=True, stop=True),
                    waits=("ld_bv", ("wre", g, 1, 7), ("kev", g, 15, 1)),
                    inc="TE", key=("bvm", g))
                P.em("vector", lambda v, BVS=BVS: v.tensor_copy(BVS[:, :], PSD[0:128, 512:768]),
                     waits=(("bvm", g),), inc="VE", key=("bve", g))
                VSL = [(PSA, 512), (PSB, 512), (PSC, 0), (PSC, 512)]
                for vt in range(32):
                    vq, vo = VSL[vt % 4]
                    w = [("wvh", g), ("bve", g), ("vz", g)] if vt == 0 else []
                    if vt >= 4:
                        w.append(("vev", g, vt - 4))
                    for cc in range(8):
                        P.em("tensor", lambda t, cc=cc, vt=vt, vq=vq, vo=vo, WVH=WVH: t.matmul(
                            vq[:, vo:vo + 256],
                            XNT[cc][:, vt * 128:(vt + 1) * 128],
                            WVH[:, cc * 256:(cc + 1) * 256],
                            start=(cc == 0), stop=(cc == 7)),
                            waits=w if cc == 0 else (), inc="TE" if cc == 7 else None,
                            key=("vpm", g, vt))
                    P.em("vector", lambda v, vt=vt, vq=vq, vo=vo, VAUG=VAUG, BVS=BVS: v.tensor_tensor(
                        VAUG[:, vt * 260:(vt + 1) * 260].rearrange(
                            "p (h d) -> p h d", d=65)[:, :, 0:64],
                        vq[:, vo:vo + 256].rearrange("p (h d) -> p h d", d=64),
                        BVS[:, :].rearrange("p (h d) -> p h d", d=64), ALU.add),
                        waits=(("vpm", g, vt),), inc="VE", key=("vev", g, vt))

                def bits_block(tag, idx, rhs_fn, WRsrc, OFFsrc, dst, dstcol,
                               logit_extra, prev_bitm, bits_extra):
                    for cc in range(8):
                        w = []
                        if cc == 0:
                            w = [("off", g, 1)] + list(logit_extra)
                        P.em("tensor", lambda t, cc=cc, rhs=rhs_fn(cc), WRsrc=WRsrc: t.matmul(
                            PSD[0:32, 0:512], WRsrc[:, cc * 32:(cc + 1) * 32], rhs,
                            start=(cc == 0), stop=(cc == 7)),
                            waits=w, inc="TE" if cc == 7 else None, key=(tag, "lgm", g, idx))
                    P.em("vector", lambda v, OFFsrc=OFFsrc: v.tensor_scalar(
                        LSB[0:32, :], PSD[0:32, 0:512], OFFsrc[:, :], None, ALU.add),
                        waits=((tag, "lgm", g, idx), "v_c0"), inc="VE", key=(tag, "lg", g, idx))
                    bufs = (LSB, LA, LB, LC)
                    for ri in range(3):
                        P.em("tensor", lambda t, ri=ri, b=bufs[ri]: t.matmul(
                            PSD[0:96, 512:1024], PERMb[0:96, ri * 96:(ri + 1) * 96],
                            b[0:96, :], start=True, stop=True),
                            waits=((tag, "lg", g, idx),) if ri == 0 else ((tag, "bfm", g, idx, ri - 1),),
                            inc="TE", key=(tag, "bfp", g, idx, ri))
                        P.em("vector", lambda v, ri=ri, b=bufs[ri], b2=bufs[ri + 1]: v.tensor_max(
                            b2[0:32, :], b[0:32, :], PSD[0:32, 512:1024]),
                            waits=((tag, "bfp", g, idx, ri),), inc="VE", key=(tag, "bfm", g, idx, ri))
                    we = [(tag, "bfm", g, idx, 2)]
                    if prev_bitm is not None:
                        we.append(prev_bitm)
                    P.em("vector", lambda v: v.tensor_tensor(
                        ONEHOT[0:32, :], LSB[0:32, :], LC[0:32, :], ALU.is_equal),
                        waits=we, inc="VE", key=(tag, "oh", g, idx))
                    for hl in range(HPQ):
                        w4 = [(tag, "oh", g, idx)]
                        if hl >= 2:
                            w4.append((tag, "bitcp", g, idx, hl - 2))
                        elif bits_extra:
                            w4 += list(bits_extra)
                        P.em("tensor", lambda t, hl=hl: t.matmul(
                            PSC[0:3, (hl % 2) * 512:(hl % 2) * 512 + 512],
                            CBb[0:96, 3 * hl:3 * hl + 3], ONEHOT[0:96, :],
                            start=True, stop=True),
                            waits=w4, inc="TE", key=(tag, "bitm", g, idx, hl))
                        P.em("vector", lambda v, hl=hl, dst=dst, c0=dstcol(hl): v.tensor_copy(
                            dst[64:67, c0:c0 + 512], PSC[0:3, (hl % 2) * 512:(hl % 2) * 512 + 512]),
                            waits=((tag, "bitm", g, idx, hl),), inc="VE",
                            key=(tag, "bitcp", g, idx, hl))

                def kb_lgm(jt):
                    for cc in range(8):
                        w = []
                        if cc == 0:
                            w = [("off", g, 1)] if jt == 0 else [("kb", "lg", g, jt - 1)]
                        P.em("tensor", lambda t, cc=cc, jt=jt, WRK=WRK: t.matmul(
                            PSD[0:32, 0:512], WRK[:, cc * 32:(cc + 1) * 32],
                            XNT[cc][:, jt * 512:(jt + 1) * 512],
                            start=(cc == 0), stop=(cc == 7)),
                            waits=w, inc="TE" if cc == 7 else None, key=("kb", "lgm", g, jt))

                def kb_lg(jt):
                    w = [("kb", "lgm", g, jt), "v_c0"]
                    LB_ = LSB if jt % 2 == 0 else LSB2
                    P.em("vector", lambda v, LB_=LB_: v.tensor_scalar(
                        LB_[0:32, :], PSD[0:32, 0:512], OFFK[:, :], None, ALU.add),
                        waits=w, inc="VE", key=("kb", "lg", g, jt))

                def kb_round(jt, ri):
                    LB_ = LSB if jt % 2 == 0 else LSB2
                    bufs = (LB_, LA, LB, LC)
                    P.em("tensor", lambda t, ri=ri, b=bufs[ri]: t.matmul(
                        PSD[0:96, 512:1024], PERMb[0:96, ri * 96:(ri + 1) * 96],
                        b[0:96, :], start=True, stop=True),
                        waits=(("kb", "lg", g, jt),) if ri == 0 else
                        (("kb", "bfm", g, jt, ri - 1),),
                        inc="TE", key=("kb", "bfp", g, jt, ri))
                    P.em("vector", lambda v, ri=ri, b=bufs[ri], b2=bufs[ri + 1]: v.tensor_max(
                        b2[0:32, :], b[0:32, :], PSD[0:32, 512:1024]),
                        waits=(("kb", "bfp", g, jt, ri),), inc="VE",
                        key=("kb", "bfm", g, jt, ri))

                def kb_tail(jt):
                    LB_ = LSB if jt % 2 == 0 else LSB2
                    we = [("kb", "bfm", g, jt, 2)]
                    if jt > 0:
                        we.append(("kb", "bitm", g, jt - 1, HPQ - 1))
                    P.em("vector", lambda v, LB_=LB_: v.tensor_tensor(
                        ONEHOT[0:32, :], LB_[0:32, :], LC[0:32, :], ALU.is_equal),
                        waits=we, inc="VE", key=("kb", "oh", g, jt))
                    for hl in range(HPQ):
                        w4 = [("kb", "oh", g, jt)]
                        if hl >= 2:
                            w4.append(("kb", "bitcp", g, jt, hl - 2))
                        elif jt > 0:
                            w4 += [("kb", "bitcp", g, jt - 1, 2), ("kb", "bitcp", g, jt - 1, 3)]
                        P.em("tensor", lambda t, hl=hl: t.matmul(
                            PSC[0:3, (hl % 2) * 512:(hl % 2) * 512 + 512],
                            CBb[0:96, 3 * hl:3 * hl + 3], ONEHOT[0:96, :],
                            start=True, stop=True),
                            waits=w4, inc="TE", key=("kb", "bitm", g, jt, hl))
                        P.em("vector", lambda v, hl=hl, jt=jt, AUGK=AUGK: v.tensor_copy(
                            AUGK[64:67, hl * SEQ + jt * 512:hl * SEQ + jt * 512 + 512],
                            PSC[0:3, (hl % 2) * 512:(hl % 2) * 512 + 512]),
                            waits=(("kb", "bitm", g, jt, hl),), inc="VE",
                            key=("kb", "bitcp", g, jt, hl))

                for jt in range(9):
                    if jt < 8:
                        kb_lgm(jt)
                    if jt >= 1:
                        for ri in range(3):
                            kb_round(jt - 1, ri)
                        kb_tail(jt - 1)
                    if jt < 8:
                        kb_lg(jt)

                for it in range(8):
                    for p in range(2):
                        qq = it * 2 + p
                        w = []
                        if qq == 0:
                            w = [("wqh", g), ("kev", g, 15, 1)]
                        if it > 0:
                            w.append(("exp", g, it - 1, 63 if p == 0 else 61))
                        for cc in range(8):
                            P.em("tensor", lambda t, cc=cc, p=p, it=it, qq=qq, WQH=WQH: t.matmul(
                                SCPS[qq % 2][:, 0:512],
                                WQH[:, cc * 256 + p * 128:cc * 256 + (p + 1) * 128],
                                XNT[cc][:, it * 512:(it + 1) * 512],
                                start=(cc == 0), stop=(cc == 7)),
                                waits=w if cc == 0 else (),
                                inc="TE" if cc == 7 else None, key=("qpm", g, qq))
                        for eo in range(2):
                            hl = 2 * p + eo
                            P.em("vector", lambda v, hl=hl, eo=eo, qq=qq, p=p, g=g, AUGQ=AUGQ:
                                 v.tensor_scalar(
                                     AUGQ[0:64, hl * 512:(hl + 1) * 512],
                                     SCPS[qq % 2][eo * 64:eo * 64 + 64, 0:512],
                                     BQTF[eo * 64:eo * 64 + 64, 2 * g + p:2 * g + p + 1],
                                     None, ALU.add),
                                 waits=(("qpm", g, qq),), inc="VE", key=("qev", g, qq, eo))
                    if it == 0:
                        lex = [("kb", "lg", g, 7)]
                        pbm = ("kb", "bitm", g, 7, HPQ - 1)
                        bex = [("kb", "bitcp", g, 7, 2), ("kb", "bitcp", g, 7, 3)]
                    else:
                        lex = [("rc", g, it - 1, HPQ - 1), ("exp", g, it - 1, 62)]
                        pbm = ("qb", "bitm", g, it - 1, HPQ - 1)
                        bex = [("cmul", g, it - 1, HPQ - 1)]
                    bits_block("qb", it,
                               lambda cc, it=it: XNT[cc][:, it * 512:(it + 1) * 512],
                               WRQ, OFFQ, AUGQ, lambda hl: hl * 512,
                               logit_extra=lex, prev_bitm=pbm, bits_extra=bex)
                    LAG = 3
                    SLOT = (PSA, PSB, PSD)

                    def emit_scores(lin, it=it, g=g):
                        hl, jj = divmod(lin, 16)
                        w5 = []
                        if lin == 0:
                            w5 = [("qev", g, it * 2 + 1, 1),
                                  ("qb", "bitcp", g, it, HPQ - 1),
                                  ("kb", "bitcp", g, 7, HPQ - 1),
                                  ("kev", g, 15, 1), ("vev", g, 31)]
                        if lin >= LAG:
                            w5.append(("exp", g, it, lin - LAG))
                        elif it > 0:
                            w5.append(("exp", g, it - 1, 61 + lin))
                        if lin == 2:
                            w5.append(("qb", "bfm", g, it, 2))
                        for sub in range(2):
                            j = 2 * jj + sub
                            P.em("tensor", lambda t, hl=hl, j=j, lin=lin, sub=sub,
                                 AUGK=AUGK, AUGQ=AUGQ: t.matmul(
                                     SLOT[lin % 3][:, sub * 512:sub * 512 + 512],
                                     AUGK[0:96, hl * SEQ + j * 128: hl * SEQ + (j + 1) * 128],
                                     AUGQ[0:96, hl * 512:(hl + 1) * 512],
                                     start=True, stop=True),
                                 waits=w5 if sub == 0 else (),
                                 inc="TE" if sub == 1 else None, key=("scm", g, it, lin))
                        we2 = [("scm", g, it, lin)]
                        if lin >= 8:
                            we2.append(("pvm", g, it, lin - 8))
                        elif it > 0:
                            we2.append(("pvm", g, it - 1, 56 + lin))
                        P.em("scalar", lambda s, lin=lin: s.activation(
                            EXPS[lin % 8][:, :], SLOT[lin % 3][:, :], AF.Exp, bias=EXP_BIAS),
                            waits=we2, inc="SE", key=("exp", g, it, lin))

                    def emit_pv(plin, it=it, g=g):
                        hl, jjp = divmod(plin, 16)
                        for sub in range(2):
                            j = 2 * jjp + sub
                            w6 = [("exp", g, it, plin)] if sub == 0 else []
                            if jjp == 0 and sub == 0 and (it > 0 or hl >= 2):
                                pit, phl = (it, hl - 2) if hl >= 2 else (it - 1, hl + 2)
                                w6.append(("cmul", g, pit, phl))
                            P.em("tensor", lambda t, hl=hl, j=j, plin=plin, sub=sub,
                                 VAUG=VAUG: t.matmul(
                                     PSC[0:65, (hl % 2) * 512:(hl % 2) * 512 + 512],
                                     VAUG[:, (j * HPQ + hl) * 65:(j * HPQ + hl) * 65 + 65],
                                     EXPS[plin % 8][:, sub * 512:sub * 512 + 512],
                                     start=(jjp == 0 and sub == 0), stop=(jjp == 15 and sub == 1)),
                                 waits=w6, inc="TE" if sub == 1 else None,
                                 key=("pvm", g, it, plin))
                        if jjp == 15:
                            emit_tail(hl)

                    def emit_tail(hl, it=it, g=g):
                        par = hl % 2
                        wdn = [("pvm", g, it, hl * 16 + 15)]
                        if debug and (g, it, hl) == (0, 7, 1):
                            wdn.append("dbg_dn_st")
                        P.em("vector", lambda v, hl=hl, par=par: v.tensor_copy(
                            DNBF[par][0:1, :], PSC[64:65, (hl % 2) * 512:(hl % 2) * 512 + 512]),
                            waits=wdn, inc="VE", key=("dn", g, it, hl))
                        if debug and (g, it, hl) == (0, 7, 0):
                            P.em("sync", lambda sy: sy.dma_start(d_ddn, DNBF[0][:, :]),
                                 waits=(("dn", 0, 7, 0),), inc="Cdbg", inck=16, key="dbg_dn_st")
                            P.em("vector", lambda v: v.tensor_copy(RCF[0][:, :], PSC[0:64, 0:512]),
                                 waits=(("pvm", 0, 7, 15),), inc="VE", key="dbg_pv_ev")
                            P.em("sync", lambda sy: sy.dma_start(d_dpv, RCF[0][:, :]),
                                 waits=("dbg_pv_ev",), inc="Cdbg", inck=16, key="dbg_pv_st")
                        w7 = [("dn", g, it, hl)]
                        P.em("tensor", lambda t, hl=hl, par=par: t.matmul(
                            PSC[64:128, (hl % 2) * 512:(hl % 2) * 512 + 512],
                            ONESB[0:1, 0:64], DNBF[par][0:1, :],
                            start=True, stop=True),
                            waits=w7, inc="TE", key=("bcm", g, it, hl))
                        wbe = [("bcm", g, it, hl)]
                        if it * 4 + hl >= 2:
                            k3 = it * 4 + hl - 2
                            wbe.append(("rc", g, k3 // 4, k3 % 4))
                        P.em("vector", lambda v, hl=hl, par=par: v.tensor_copy(
                            BCS[par][:, :], PSC[64:128, (hl % 2) * 512:(hl % 2) * 512 + 512]),
                            waits=wbe, inc="VE", key=("bce", g, it, hl))

                    def emit_tail2(hl, it=it, g=g):
                        par = hl % 2
                        HG = 4 * g + hl
                        pr = (hl // 2) % 2
                        wrl = [("bce", g, it, hl)]
                        if debug and (g, it, hl) == (0, 7, 0):
                            wrl.append("dbg_pv_st")
                        P.em("scalar", lambda s, par=par: s.activation(
                            RCF[par][:, :], BCS[par][:, :], AF.Ln),
                            waits=wrl, inc="SE", key=("rcl", g, it, hl))
                        P.em("scalar", lambda s, par=par: s.activation(
                            BCS[par][:, :], RCF[par][:, :], AF.Exp, scale=np.float32(-1.0)),
                            waits=(("rcl", g, it, hl),), inc="SE", key=("rc", g, it, hl))
                        if debug and (g, it, hl) == (0, 7, 0):
                            P.em("sync", lambda sy: sy.dma_start(d_dbcs, BCS[0][:, :]),
                                 waits=(("rc", 0, 7, 0),), inc="Cdbg", inck=16, key="dbg_bcs_st")
                        w8 = [("rc", g, it, hl)]
                        if it * 4 + hl >= 4:
                            k2 = it * 4 + hl - 4
                            w8.append(("cst", g, k2 // 4, (k2 % 4) // 2))
                        P.em("vector", lambda v, hl=hl, pr=pr, par=par: v.tensor_tensor(
                            CATS[pr][(hl % 2) * 64:(hl % 2) * 64 + 64, :],
                            PSC[0:64, (hl % 2) * 512:(hl % 2) * 512 + 512],
                            BCS[par][0:64, :], ALU.mult),
                            waits=w8, inc="VE", key=("cmul", g, it, hl))
                        if hl % 2 == 1:
                            ch = HG // 2
                            P.em("sync", lambda sy, ch=ch, it=it, pr=pr: sy.dma_start(
                                d_cat[ch * 128:(ch + 1) * 128, it * 512:(it + 1) * 512],
                                CATS[pr][:, :]),
                                waits=(("cmul", g, it, hl),), inc=f"Ccst{pr}", inck=16,
                                key=("cst", g, it, hl // 2))

                    pend = []
                    for lin in range(64 + LAG + 4):
                        if lin < 64:
                            emit_scores(lin)
                        if LAG <= lin < 64 + LAG:
                            emit_pv(lin - LAG)
                        if pend and pend[0][0] <= lin:
                            emit_tail2(pend.pop(0)[1])
                        if lin >= LAG and (lin - LAG) % 16 == 15:
                            pend.append((lin + 4, (lin - LAG) // 16))
                    while pend:
                        emit_tail2(pend.pop(0)[1])
            if debug and g == 0:
                P.em("sync", lambda sy, AUGK=AUGK: sy.dma_start(d_daugk, AUGK[:, :]),
                     inc="ST", inck=16)
                P.em("sync", lambda sy, VAUG=VAUG: sy.dma_start(d_dvaug, VAUG[:, :]),
                     inc="ST", inck=16)
                P.em("sync", lambda sy, AUGQ=AUGQ: sy.dma_start(d_daugq, AUGQ[:, :]),
                     inc="ST", inck=16)
            P.barrier()
        xnt_scope.close()

        pre = ExitStack()
        with pre:
            WGI = [pre.enter_context(nc.sbuf_tensor(f"wgi{i}", [128, 1024], BF16)) for i in range(4)]
            TST = [pre.enter_context(nc.sbuf_tensor(f"tst{i}", [128, 2048], BF16)) for i in range(2)]
            WDI = [pre.enter_context(nc.sbuf_tensor(f"wdi{i}", [128, 3072], BF16)) for i in range(4)]
            TSD = [pre.enter_context(nc.sbuf_tensor(f"tsd{i}", [128, 128], BF16)) for i in range(6)]

            TSL = [(PSC, 0), (PSA, 0), (PSB, 0), (PSD, 0),
                   (PSC, 512), (PSA, 512), (PSB, 512), (PSD, 512)]
            tser = [0]
            kkd = [0]

            def wgu_unit(li):
                hc, half = divmod(li, 2)
                wsrc = d_wg if half == 0 else d_wu
                w = [("pgt", li - 4, 7)] if li >= 4 else []
                P.em("gpsimd", lambda gp, hc=hc, li=li, wsrc=wsrc: gp.dma_start(
                    WGI[li % 4][:, :], wsrc[hc * 128:(hc + 1) * 128, :]),
                    waits=w, inc=f"Ggi{li % 4}", inck=16, key=("pgl", li))
                for cb in range(8):
                    ts = tser[0]
                    pt, po = TSL[ts % 8]
                    w2 = [("pgl", li)] if cb == 0 else []
                    if li == 0 and cb == 0:
                        w2.append("ld_idb")
                    if ts >= 8:
                        w2.append(("tev", ts - 8))
                    P.em("tensor", lambda t, li=li, cb=cb, pt=pt, po=po: t.matmul(
                        pt[:, po:po + 128],
                        WGI[li % 4][:, cb * 128:(cb + 1) * 128], IDENTB[:, :],
                        start=True, stop=True),
                        waits=w2, inc="TE", key=("pgt", li, cb))
                    w3 = [("pgt", li, cb)]
                    if cb == 0 and half == 0 and hc >= 2:
                        w3.append(("pgs", hc - 2))
                    P.em("vector", lambda v, half=half, cb=cb, li=li, pt=pt, po=po: v.tensor_copy(
                        TST[(li // 2) % 2][:, half * 1024 + cb * 128: half * 1024 + (cb + 1) * 128],
                        pt[:, po:po + 128]),
                        waits=w3, inc="VE", key=("tev", ts))
                    tser[0] += 1
                if half == 1:
                    P.em("sync", lambda sy, hc=hc, ts=tser[0] - 1: sy.dma_start(
                        d_wgu[hc * 128:(hc + 1) * 128, :], TST[hc % 2][:, :]),
                        waits=(("tev", ts),), inc=f"Ctst{hc % 2}", inck=16, key=("pgs", hc))

            def wd_unit(rb):
                w = [("pdt", rb - 4, 23)] if rb >= 4 else []
                P.em("gpsimd", lambda gp, rb=rb: gp.dma_start(
                    WDI[rb % 4][:, :], d_wd[rb * 128:(rb + 1) * 128, :]),
                    waits=w, inc=f"Gdi{rb % 4}", inck=16, key=("pdl", rb))
                for hb in range(24):
                    ts = tser[0]
                    kk = kkd[0]
                    pt, po = TSL[ts % 8]
                    w2 = [("pdl", rb)] if hb == 0 else []
                    if ts >= 8:
                        w2.append(("tev", ts - 8))
                    P.em("tensor", lambda t, rb=rb, hb=hb, pt=pt, po=po: t.matmul(
                        pt[:, po:po + 128],
                        WDI[rb % 4][:, hb * 128:(hb + 1) * 128], IDENTB[:, :],
                        start=True, stop=True),
                        waits=w2, inc="TE", key=("pdt", rb, hb))
                    w3 = [("pdt", rb, hb)]
                    if kk >= 6:
                        w3.append(("pds", kk - 6))
                    P.em("vector", lambda v, hb=hb, kk=kk, pt=pt, po=po: v.tensor_copy(
                        TSD[kk % 6][:, :], pt[:, po:po + 128]),
                        waits=w3, inc="VE", key=("tev", ts))
                    P.em("sync", lambda sy, rb=rb, hb=hb, kk=kk: sy.dma_start(
                        d_wdt[hb * 128:(hb + 1) * 128, rb * 128:(rb + 1) * 128],
                        TSD[kk % 6][:, :]),
                        waits=(("tev", ts),), inc=f"Ctsd{kk % 6}", inck=16, key=("pds", kk))
                    tser[0] += 1
                    kkd[0] += 1

            for grp in range(8):
                for li in range(grp * 6, grp * 6 + 6):
                    wgu_unit(li)
                wd_unit(grp)
        P.barrier()

        ds = ExitStack()
        with ds:
            DT = ds.enter_context
            XB = [DT(nc.sbuf_tensor("xb0", [128, 4096], F32))]
            WGUC = [DT(nc.sbuf_tensor(f"wguc{i}", [128, 2048], BF16)) for i in range(4)]
            GSB = [DT(nc.sbuf_tensor(f"gsb{i}", [128, 512], F32)) for i in range(2)]
            HTS = DT(nc.sbuf_tensor("hts", [128, 24 * 512], BF16))
            WDTS = DT(nc.sbuf_tensor("wdts", [128, 24 * 1024], BF16))
            WOSB = DT(nc.sbuf_tensor("wosb", [128, 8 * 1024], BF16))
            CTC = [DT(nc.sbuf_tensor("ctc0", [128, 4096], BF16))]
            OSB = [DT(nc.sbuf_tensor("osb0", [128, 1024], F32))]
            DSTAT = DT(nc.sbuf_tensor("dstat_s", [128, 256], F32)) if debug else None

            for q in range(4):
                P.em("sync", lambda sy, q=q: sy.dma_start(
                    WDTS[:, q * 6144:(q + 1) * 6144].rearrange("p (hc e) -> p hc e", e=1024),
                    d_wdt[q * 768 * 1024 // 1024 * 1024 // 1024 * 0 + q * 6 * 128:(q + 1) * 6 * 128, :]
                    .rearrange("(hc p) e -> p hc e", p=128)),
                    inc=f"Cwdt{q}", inck=16, key=("wdl", q))
            P.em("gpsimd", lambda gp: gp.dma_start(
                WOSB[:, :].rearrange("p (ch e) -> p ch e", e=1024),
                d_w["wo"].rearrange("(ch p) e -> p ch e", p=128)),
                inc="Gwo", inck=16, key="wol")

            for ib in range(8):
                P.em("sync", lambda sy, ib=ib: sy.dma_start(
                    CTC[0][:, :].rearrange("p (ch i) -> p ch i", i=512),
                    d_cat[:, ib * 512:(ib + 1) * 512].rearrange("(ch p) i -> p ch i", p=128)),
                    waits=(("fev", ib - 1, 3),) if ib >= 1 else (), inc="Cctc0", inck=16,
                    key=("ctl", ib))
                w = [("fev", ib - 1, s_) for s_ in range(4)] if ib >= 1 else []
                P.em("sync", lambda sy, ib=ib: sy.dma_start(
                    XB[0][:, :].rearrange("p (st d) -> p st d", d=1024),
                    d_x[ib * 512:(ib + 1) * 512, :].rearrange("(st p) d -> p st d", p=128)),
                    waits=w, inc="Cxd0", inck=16, key=("D", "xload", ib))
                for hc in range(24):
                    lk = ib * 24 + hc
                    w = [("gum", lk - 4)] if lk >= 4 else []
                    P.em("gpsimd", lambda gp, hc=hc, lk=lk: gp.dma_start(
                        WGUC[lk % 4][:, :], d_wgu[hc * 128:(hc + 1) * 128, :]),
                        waits=w, inc=f"Ggu{lk % 4}", inck=16, key=("gul", lk))
                    wg_ = [("gul", lk)]
                    if lk >= 2:
                        wg_.append(("hts", lk - 2))
                    for cc in range(8):
                        P.em("tensor", lambda t, hc=hc, cc=cc, lk=lk, ib=ib: t.matmul(
                            SCPS[hc % 2][:, 0:512],
                            WGUC[lk % 4][:, cc * 128:(cc + 1) * 128],
                            XNT[cc][:, ib * 512:(ib + 1) * 512],
                            start=(cc == 0), stop=(cc == 7)),
                            waits=wg_ if cc == 0 else ())
                    for cc in range(8):
                        P.em("tensor", lambda t, hc=hc, cc=cc, lk=lk, ib=ib: t.matmul(
                            SCPS[hc % 2][:, 512:1024],
                            WGUC[lk % 4][:, 1024 + cc * 128:1024 + (cc + 1) * 128],
                            XNT[cc][:, ib * 512:(ib + 1) * 512],
                            start=(cc == 0), stop=(cc == 7)),
                            inc="TE" if cc == 7 else None, key=("gum", lk))
                    ws = [("gum", lk)]
                    if lk >= 2:
                        ws.append(("hts", lk - 2))
                    P.em("scalar", lambda s, hc=hc, lk=lk: s.activation(
                        GSB[hc % 2][:, :], SCPS[hc % 2][:, 0:512], AF.Silu, scale=C127),
                        waits=ws, inc="SE", key=("silu", lk))
                    wh = [("silu", lk)]
                    if ib > 0:
                        wh.append(("fin", ib * 4 - 1))
                    if DBG_XNTB[0] and lk == 24:
                        wh.append("dbg_hts_st")
                    P.em("vector", lambda v, hc=hc, lk=lk: v.scalar_tensor_tensor(
                        HTS[:, hc * 512:(hc + 1) * 512], SCPS[hc % 2][:, 512:1024],
                        C127SQ, GSB[hc % 2][:, :], ALU.mult, ALU.mult),
                        waits=wh, inc="VE", key=("hts", lk))
                if debug and ib == 0:
                    P.em("sync", lambda sy: sy.dma_start(d_dhts, HTS[:, :]),
                         waits=(("hts", 23),), inc="Cdbg", inck=16, key="dbg_hts_st")
                for st in range(4):
                    tt = ib * 4 + st
                    w = [("hts", ib * 24 + 23), ("ctl", ib), "wol"]
                    if tt == 0:
                        w += [("wdl", q_) for q_ in range(4)]
                    if tt >= 1:
                        w.append(("fev", (tt - 1) // 4, (tt - 1) % 4))
                    for eh in range(2):
                        P.em("tensor", lambda t, eh=eh: t.matmul(
                            PSD[:, eh * 512:(eh + 1) * 512], ONESB[0:1, 0:128],
                            BOROW[0:1, eh * 512:(eh + 1) * 512], start=True, stop=False),
                            waits=w if eh == 0 else ())
                    for ch in range(8):
                        for eh in range(2):
                            P.em("tensor", lambda t, ch=ch, eh=eh, ib=ib, st=st: t.matmul(
                                PSD[:, eh * 512:(eh + 1) * 512],
                                CTC[0][:, ch * 512 + st * 128: ch * 512 + (st + 1) * 128],
                                WOSB[:, ch * 1024 + eh * 512: ch * 1024 + (eh + 1) * 512],
                                start=False, stop=False))
                    for hc in range(24):
                        for eh in range(2):
                            last = (hc == 23)
                            P.em("tensor", lambda t, hc=hc, eh=eh, st=st, last=last: t.matmul(
                                PSD[:, eh * 512:(eh + 1) * 512],
                                HTS[:, hc * 512 + st * 128: hc * 512 + (st + 1) * 128],
                                WDTS[:, hc * 1024 + eh * 512: hc * 1024 + (eh + 1) * 512],
                                start=False, stop=last),
                                inc="TE" if (last and eh == 1) else None, key=("fin", tt))
                    wf = [("fin", tt), ("D", "xload", ib)]
                    if tt >= 1:
                        wf.append(("ost", tt - 1))
                    P.em("vector", lambda v, ib=ib, st=st, tt=tt: v.tensor_tensor(
                        OSB[0][:, :], PSD[:, :], XB[0][:, st * 1024:(st + 1) * 1024],
                        ALU.add),
                        waits=wf, inc="VE", key=("fev", ib, st))
                    P.em("sync", lambda sy, tt=tt: sy.dma_start(
                        d_out[tt * 128:(tt + 1) * 128, :], OSB[0][:, :]),
                        waits=(("fev", tt // 4, tt % 4),), inc="Cosb0", inck=16, key=("ost", tt))

            if debug:
                P.em("sync", lambda sy: sy.dma_start(d_dstat, SREG[:, 0:64].rearrange(
                    "p c -> p c")), waits=(("fev", 7, 3),), inc="Cdbg", inck=16)
            fin_waits = tuple(("LIT", s, P.cnt[s]) for s in list(P.cnt)
                              if s.startswith(("Cosb", "Ccst", "Ctst", "Ctsd")))
            P.em("sync", lambda sy: None, waits=fin_waits)

        P.emit(block)
    return nc


_NC_CACHE = None


def kernel(**inputs):
    global _NC_CACHE
    ident, cb, perms = _host_consts()
    x = np.ascontiguousarray(np.asarray(inputs["L0_attn_x"], np.float32))
    base = {
        "wq": np.ascontiguousarray(np.asarray(inputs["L0_attn_Wq"], np.float32)),
        "wk": np.ascontiguousarray(np.asarray(inputs["L0_attn_Wk"], np.float32)),
        "wv": np.ascontiguousarray(np.asarray(inputs["L0_attn_Wv"], np.float32)),
        "wo": np.ascontiguousarray(np.asarray(inputs["L0_attn_Wo"], np.float32)),
        "bq": np.ascontiguousarray(np.asarray(inputs["L0_attn_bq"], np.float32)),
        "bk": np.ascontiguousarray(np.asarray(inputs["L0_attn_bk"], np.float32)),
        "bv": np.ascontiguousarray(np.asarray(inputs["L0_attn_bv"], np.float32)),
        "bo": np.ascontiguousarray(np.asarray(inputs["L0_attn_bo"], np.float32)),
        "r": np.ascontiguousarray(np.asarray(inputs["L0_attn_R"], np.float32)),
        "wg": np.ascontiguousarray(np.asarray(inputs["L0_ffn_w_gate"], np.int32)),
        "wu": np.ascontiguousarray(np.asarray(inputs["L0_ffn_w_up"], np.int32)),
        "wd": np.ascontiguousarray(np.asarray(inputs["L0_ffn_w_down"], np.int32)),
        "identf": ident, "cbc": cb, "permsc": perms,
    }
    if _NC_CACHE is None:
        _NC_CACHE = build_program()
    nc = _NC_CACHE
    in_maps = [dict(base, x=np.ascontiguousarray(x[b])) for b in range(8)]
    res = run_bass_kernel_spmd(nc, in_maps, core_ids=list(range(8)))
    out = np.stack([np.asarray(res.results[b]["out"], np.float32) for b in range(8)], axis=0)
    return out


if __name__ == "__main__":
    nc = build_program()
    print("built ok; instructions:", nc.n_instructions())


# version 24
# speedup vs baseline: 1.1289x; 1.1289x over previous
import numpy as np
import ml_dtypes
import concourse.bass as bass
import concourse.mybir as mybir
from concourse.bass_utils import run_bass_kernel_spmd
from contextlib import ExitStack

F32 = mybir.dt.float32
BF16 = mybir.dt.bfloat16
I32 = mybir.dt.int32
AF = mybir.ActivationFunctionType
ALU = mybir.AluOpType
AX = mybir.AxisListType

SEQ = 4096
D = 1024
DK = 64
NH = 8
HPQ = 4
NQ = 4
HID = 3072
S_BIT = 16.0
EXP_BIAS = np.float32(-3.0 * S_BIT * S_BIT)
C127 = np.float32(1.0 / 127.0)
C127SQ = float(np.float32(1.0 / (127.0 * 127.0)))
RD = 1.0 / D
F_EIGHTH = 0.125

ENGINES = ("sync", "gpsimd", "tensor", "scalar", "vector")


def _host_consts():
    ident = np.eye(128, dtype=np.float32)
    codes = np.array([[S_BIT if (n >> b) & 1 else -S_BIT for b in range(3)]
                      for n in range(NH)], np.float32)
    cb = np.zeros((96, 12), np.float32)
    for h in range(HPQ):
        for n in range(NH):
            cb[8 * h + n, 3 * h: 3 * h + 3] = codes[n]
    perms = np.zeros((96, 288), np.float32)
    for ri, dd in enumerate((1, 2, 4)):
        for h in range(HPQ):
            for n in range(NH):
                perms[8 * h + n, ri * 96 + 8 * h + (n ^ dd)] = 1.0
    return ident, cb, perms


class Prog:

    def __init__(self, nc, es):
        self.nc = nc
        self.es = es
        self.m = {}
        self.cnt = {}
        self.sems = {}
        for k in ("TE", "SE", "VE"):
            self._mk(k)
        self.seq = {e: [] for e in ENGINES}

    def _mk(self, k):
        if k not in self.sems:
            self.sems[k] = self.es.enter_context(self.nc.semaphore(k))
            self.cnt[k] = 0

    def em(self, eng, fn, waits=(), inc=None, inck=1, key=None):
        if inc is not None:
            self._mk(inc)
            self.cnt[inc] += inck
            if key is not None:
                assert key not in self.m, f"duplicate milestone key {key}"
                self.m[key] = (inc, self.cnt[inc])
        self.seq[eng].append((fn, tuple(waits), inc, inck))

    def barrier(self):
        snap = [(s, v) for s, v in self.cnt.items() if v]
        for e in ENGINES:
            self.seq[e].append((None, tuple(("LIT", s, v) for s, v in snap), None, 1))

    def emit(self, block):
        sems, m = self.sems, self.m
        for e in ENGINES:
            ops = self.seq[e]

            def body(eng, ops=ops):
                for fn, waits, inc, inck in ops:
                    for w in waits:
                        if isinstance(w, tuple) and len(w) == 3 and w[0] == "LIT":
                            eng.wait_ge(sems[w[1]], w[2])
                        else:
                            sname, val = m[w]
                            eng.wait_ge(sems[sname], val)
                    if fn is not None:
                        inst = fn(eng)
                        if inc is not None:
                            inst.then_inc(sems[inc], inck)
            getattr(block, e)(body)


def build_program(debug=False):
    nc = bass.Bass("TRN2", target_bir_lowering=False)
    d_x = nc.dram_tensor("x", [SEQ, D], F32, kind="ExternalInput").ap()
    d_w = {n: nc.dram_tensor(n, [D, D], F32, kind="ExternalInput").ap()
           for n in ("wq", "wk", "wv", "wo")}
    d_b = {n: nc.dram_tensor(n, [D], F32, kind="ExternalInput").ap()
           for n in ("bq", "bk", "bv", "bo")}
    d_r = nc.dram_tensor("r", [NH, DK], F32, kind="ExternalInput").ap()
    d_wg = nc.dram_tensor("wg", [HID, D], I32, kind="ExternalInput").ap()
    d_wu = nc.dram_tensor("wu", [HID, D], I32, kind="ExternalInput").ap()
    d_wd = nc.dram_tensor("wd", [D, HID], I32, kind="ExternalInput").ap()
    d_ident = nc.dram_tensor("identf", [128, 128], F32, kind="ExternalInput").ap()
    d_cb = nc.dram_tensor("cbc", [96, 12], F32, kind="ExternalInput").ap()
    d_perms = nc.dram_tensor("permsc", [96, 288], F32, kind="ExternalInput").ap()
    d_out = nc.dram_tensor("out", [SEQ, D], F32, kind="ExternalOutput").ap()
    _scr = "ExternalOutput" if debug else "Internal"
    d_cat = nc.dram_tensor("catscr", [D, SEQ], BF16, kind=_scr).ap()
    d_wgu = nc.dram_tensor("wguscr", [24 * 128, 2048], BF16, kind=_scr).ap()
    d_wdt = nc.dram_tensor("wdtscr", [24 * 128, 1024], BF16, kind=_scr).ap()
    if debug:
        d_dxnt = nc.dram_tensor("dxnt", [1024, SEQ], BF16, kind="ExternalOutput").ap()
        d_daugk = nc.dram_tensor("daugk", [128, HPQ * SEQ], BF16, kind="ExternalOutput").ap()
        d_dvaug = nc.dram_tensor("dvaug", [128, 32 * HPQ * 65], BF16, kind="ExternalOutput").ap()
        d_daugq = nc.dram_tensor("daugq", [128, HPQ * 512], BF16, kind="ExternalOutput").ap()
        d_dexp = nc.dram_tensor("dexp", [128, 1024], BF16, kind="ExternalOutput").ap()
        d_ddn = nc.dram_tensor("ddn", [1, 512], BF16, kind="ExternalOutput").ap()
        d_dbcs = nc.dram_tensor("dbcs", [64, 512], F32, kind="ExternalOutput").ap()
        d_dpv = nc.dram_tensor("dpv", [64, 512], F32, kind="ExternalOutput").ap()
        d_dstat = nc.dram_tensor("dstat", [128, 64], F32, kind="ExternalOutput").ap()
        d_dxntb = nc.dram_tensor("dxntb", [128, 4096], BF16, kind="ExternalOutput").ap()
        d_dhts = nc.dram_tensor("dhts", [128, 12288], BF16, kind="ExternalOutput").ap()

    DBG_XNTB = [debug]
    es = ExitStack()
    with es:
        ET = es.enter_context
        P = Prog(nc, es)

        PSA = ET(nc.psum_tensor("PSA", [128, 1024], F32))
        PSB = ET(nc.psum_tensor("PSB", [128, 1024], F32))
        PSC = ET(nc.psum_tensor("PSC", [128, 1024], F32))
        PSD = ET(nc.psum_tensor("PSD", [128, 1024], F32))
        SCPS = (PSA, PSB)
        TRB = [(PSC, 0), (PSA, 0), (PSB, 0), (PSC, 512), (PSA, 512), (PSB, 512)]

        IDENTF = ET(nc.sbuf_tensor("identf_s", [128, 128], F32))
        IDENTB = ET(nc.sbuf_tensor("identb_s", [128, 128], BF16))
        CBb = ET(nc.sbuf_tensor("cbb", [96, 12], BF16))
        RS = ET(nc.sbuf_tensor("rs", [128, 8], BF16))
        ONESB = ET(nc.sbuf_tensor("onesb", [1, 512], BF16))
        BQTF = ET(nc.sbuf_tensor("bqtf", [128, 8], F32))
        BKTF = ET(nc.sbuf_tensor("bktf", [128, 8], F32))
        BQTB = ET(nc.sbuf_tensor("bqtb", [128, 8], BF16))
        BKTB = ET(nc.sbuf_tensor("bktb", [128, 8], BF16))
        BOROW = ET(nc.sbuf_tensor("borow", [1, D], BF16))
        BVROW = ET(nc.sbuf_tensor("bvrow", [1, D], BF16))
        LNS = [ET(nc.sbuf_tensor(f"lns{i}", [128, 8], F32)) for i in range(2)]
        SREG = ET(nc.sbuf_tensor("sreg", [128, 64], F32))
        P0T = ET(nc.sbuf_tensor("p0t", [8, 128], F32))
        P0R = ET(nc.sbuf_tensor("p0r", [8, 64], F32))

        xnt_scope = ExitStack()
        XS = xnt_scope.enter_context
        XNT = [ET(nc.sbuf_tensor(f"xnt{c}", [128, SEQ], BF16)) for c in range(8)]
        PERMb = XS(nc.sbuf_tensor("permb", [96, 288], F32))
        ONEHOT = XS(nc.sbuf_tensor("onehot", [96, 512], BF16))
        LSB = XS(nc.sbuf_tensor("lsb", [96, 512], F32))
        LSB2 = XS(nc.sbuf_tensor("lsb2", [96, 512], F32))
        LA = XS(nc.sbuf_tensor("la", [96, 512], F32))
        LB = XS(nc.sbuf_tensor("lb", [96, 512], F32))
        LC = XS(nc.sbuf_tensor("lc", [96, 512], F32))
        OFFQ = XS(nc.sbuf_tensor("offq", [32, 1], F32))
        OFFK = XS(nc.sbuf_tensor("offk", [32, 1], F32))
        EXPS = [XS(nc.sbuf_tensor(f"exps{i}", [128, 1024], BF16)) for i in range(8)]
        DNBF = [XS(nc.sbuf_tensor(f"dnbf{i}", [1, 512], BF16)) for i in range(2)]
        BCS = [XS(nc.sbuf_tensor(f"bcs{i}", [64, 512], F32)) for i in range(2)]
        RCF = [XS(nc.sbuf_tensor(f"rcf{i}", [64, 512], F32)) for i in range(2)]
        CATS = [XS(nc.sbuf_tensor(f"cats{i}", [128, 512], BF16)) for i in range(2)]

        a_scope = ExitStack()
        AS = a_scope.enter_context
        XNF = [AS(nc.sbuf_tensor(f"xnf{i}", [128, 1024], F32)) for i in range(2)]
        SQS = AS(nc.sbuf_tensor("sqs", [128, 1024], F32))
        XT = [AS(nc.sbuf_tensor(f"xt{i}", [128, 1024], F32)) for i in range(3)]

        block = ET(nc.Block())

        P.em("sync", lambda sy: sy.dma_start(IDENTF[:, :], d_ident), inc="Cmisc", inck=16, key="ld_id")
        P.em("sync", lambda sy: sy.dma_start(P0T[:, :], d_b["bq"].rearrange("(t p) -> t p", p=128)),
             inc="Cmisc", inck=16, key="ld_bq")
        P.em("sync", lambda sy: sy.dma_start(P0R[:, :], d_r), inc="Cmisc", inck=16, key="ld_r")
        P.em("gpsimd", lambda gp: gp.dma_start(IDENTB[:, :], d_ident), inc="Gmisc", inck=16,
             key="ld_idb")
        P.em("gpsimd", lambda gp: gp.dma_start(CBb[:, :], d_cb), inc="Gmisc", inck=16)
        P.em("gpsimd", lambda gp: gp.dma_start(PERMb[:, :], d_perms), inc="Gmisc", inck=16)
        P.em("gpsimd", lambda gp: gp.dma_start(BOROW[:, :], d_b["bo"].rearrange("(a e) -> a e", a=1)),
             inc="Gmisc", inck=16)
        P.em("gpsimd", lambda gp: gp.dma_start(BVROW[:, :], d_b["bv"].rearrange("(a e) -> a e", a=1)),
             inc="Gmisc", inck=16, key="ld_bv")
        P.em("tensor", lambda t: t.transpose(PSA[0:128, 0:8], P0T[:, :], IDENTF[0:8, 0:8]),
             waits=("ld_r",), inc="TE", key="t_bq")
        P.em("tensor", lambda t: t.transpose(PSA[0:64, 512:520], P0R[:, :], IDENTF[0:8, 0:8]),
             waits=("ld_r",), inc="TE", key="t_r")
        P.em("vector", lambda v: v.tensor_copy(BQTF[:, :], PSA[0:128, 0:8]), waits=("t_bq",))
        P.em("vector", lambda v: v.tensor_copy(BQTB[:, :], PSA[0:128, 0:8]))
        P.em("vector", lambda v: v.tensor_copy(RS[0:64, :], PSA[0:64, 512:520]), waits=("t_r",))
        P.em("vector", lambda v: v.tensor_copy(RS[64:128, :], PSA[0:64, 512:520]),
             inc="VE", key="v_bq")
        P.em("sync", lambda sy: sy.dma_start(P0T[:, :], d_b["bk"].rearrange("(t p) -> t p", p=128)),
             waits=("v_bq",), inc="Cmisc", inck=16, key="ld_bk")
        P.em("tensor", lambda t: t.transpose(PSB[0:128, 0:8], P0T[:, :], IDENTF[0:8, 0:8]),
             waits=("ld_bk",), inc="TE", key="t_bk")
        P.em("vector", lambda v: v.tensor_copy(BKTF[:, :], PSB[0:128, 0:8]), waits=("t_bk",))
        P.em("vector", lambda v: v.tensor_copy(BKTB[:, :], PSB[0:128, 0:8]))
        P.em("vector", lambda v: v.memset(ONESB[:, :], 1.0))
        P.em("vector", lambda v: v.memset(ONEHOT[:, :], 0.0))
        P.em("vector", lambda v: v.memset(LSB[:, :], 0.0))
        P.em("vector", lambda v: v.memset(LSB2[:, :], 0.0))
        P.em("vector", lambda v: v.memset(LA[:, :], 0.0))
        P.em("vector", lambda v: v.memset(LB[:, :], 0.0))
        P.em("vector", lambda v: v.memset(LC[:, :], 0.0), inc="VE", key="v_c0")
        P.barrier()

        def ln_stats(tag, it, par, xsrc):
            L = LNS[par]
            P.em("vector", lambda v, x=xsrc, L=L: v.reduce_sum(L[:, 0:1], x, AX.X),
                waits=((tag, "xload", it),), inc="VE", key=(tag, "red", it))
            wm = ((tag, "red", it),)
            if it >= 2:
                wm = wm + ((tag, "apply", it - 2),)
            P.em("vector", lambda v, L=L: v.tensor_scalar_mul(L[:, 1:2], L[:, 0:1], RD),
                waits=wm, inc="VE", key=(tag, "mean", it))
            P.em("vector", lambda v, L=L: v.tensor_mul(L[:, 2:3], L[:, 1:2], L[:, 1:2]),
                waits=((tag, "mean", it),), inc="VE", key=(tag, "m2", it))
            wsm = ((tag, "xload", it),)
            if it >= 1:
                wsm = wsm + ((tag, "sq", it - 1),)
            P.em("gpsimd", lambda gp, x=xsrc: gp.tensor_mul(SQS[:, :], x, x),
                 waits=wsm, inc="PO", key=(tag, "sqm", it))
            P.em("vector", lambda v, L=L: v.reduce_sum(L[:, 6:7], SQS[:, :], AX.X),
                 waits=((tag, "sqm", it),), inc="VE", key=(tag, "sq", it))
            P.em("vector", lambda v, L=L: v.tensor_scalar(
                L[:, 3:4], L[:, 6:7], RD, L[:, 2:3], ALU.mult, ALU.subtract),
                waits=((tag, "sq", it), (tag, "m2", it)), inc="VE", key=(tag, "var", it))
            P.em("scalar", lambda s, L=L: s.activation(
                L[:, 4:5], L[:, 3:4], AF.Ln, bias=np.float32(1e-5)),
                waits=((tag, "var", it),), inc="SE", key=(tag, "lnv", it))
            wsa = ((tag, "lnv", it),)
            if it >= 2:
                wsa = wsa + ((tag, "apply", it - 2),)
            P.em("scalar", lambda s, L=L: s.activation(
                L[:, 5:6], L[:, 4:5], AF.Exp, scale=np.float32(-0.5)),
                waits=wsa, inc="SE", key=(tag, "sqa", it))

        def ln_apply(tag, it, par, xsrc, lpar):
            L = LNS[lpar]
            wap = ((tag, "sqa", it),)
            if it >= 2:
                wap = wap + ((tag, "trans", it * 8 - 9),)
            P.em("gpsimd", lambda gp, L=L, x=xsrc, par=par: gp.tensor_scalar(
                XNF[par][:, :], x, L[:, 1:2], L[:, 5:6], ALU.subtract, ALU.mult),
                waits=wap, inc="PO", key=(tag, "apply", it))

        def transpose_tile(tag, it, par, dst_fn, k):
            for cc in range(8):
                ts = k + cc
                pt, po = TRB[ts % 6]
                w = [(tag, "apply", it)] if cc == 0 else []
                if ts >= 6:
                    w.append((tag, "xevac", ts - 6))
                P.em("tensor", lambda t, par=par, cc=cc, pt=pt, po=po: t.transpose(
                    pt[:, po:po + 128],
                    XNF[par][:, cc * 128:(cc + 1) * 128], IDENTF[:, :]),
                    waits=w, inc="TE", key=(tag, "trans", ts))
                P.em("vector", lambda v, cc=cc, dst=dst_fn(cc), pt=pt, po=po: v.tensor_copy(
                    dst, pt[:, po:po + 128]),
                    waits=((tag, "trans", ts),), inc="VE", key=(tag, "xevac", ts))

        for it in range(33):
            if it < 32:
                p3 = it % 3
                w = [("A", "apply", it - 3), ("A", "red", it - 3)] if it >= 3 else []
                P.em("sync", lambda sy, it=it, p3=p3: sy.dma_start(
                    XT[p3][:, :], d_x[it * 128:(it + 1) * 128, :]),
                    waits=w, inc=f"CxtA{p3}", inck=16, key=("A", "xload", it))
                ln_stats("A", it, it % 2, XT[p3][:, :])
            if it >= 1:
                jt = it - 1
                ln_apply("A", jt, jt % 2, XT[jt % 3][:, :], jt % 2)
                P.em("vector", lambda v, jt=jt, lp=jt % 2: (
                    v.tensor_copy(SREG[:, jt * 2:jt * 2 + 1], LNS[lp][:, 1:2]),
                    v.tensor_copy(SREG[:, jt * 2 + 1:jt * 2 + 2], LNS[lp][:, 5:6]))[-1],
                    waits=(("A", "apply", jt),), inc="VE", key=("sreg", jt))
                transpose_tile("A", jt, jt % 2,
                               lambda cc, jt=jt: XNT[cc][:, jt * 128:(jt + 1) * 128], jt * 8)
        P.barrier()
        a_scope.close()
        if debug:
            for cc in range(8):
                P.em("sync", lambda sy, cc=cc: sy.dma_start(
                    d_dxnt[cc * 128:(cc + 1) * 128, :], XNT[cc][:, :]), inc="Cdbg", inck=16)
            P.barrier()

        for g in range(NQ):
            d0 = g * 256
            qs = ExitStack()
            with qs:
                QT = qs.enter_context
                AUGK = QT(nc.sbuf_tensor(f"augk{g}", [128, HPQ * SEQ], BF16))
                VAUG = QT(nc.sbuf_tensor(f"vaug{g}", [128, 32 * HPQ * 65], BF16))
                AUGQ = QT(nc.sbuf_tensor(f"augq{g}", [128, HPQ * 512], BF16))
                WQH = QT(nc.sbuf_tensor(f"wqh{g}", [128, 2048], BF16))
                WRQ = QT(nc.sbuf_tensor(f"wrq{g}", [128, 256], BF16))
                WRK = QT(nc.sbuf_tensor(f"wrk{g}", [128, 256], BF16))
                WKH = QT(nc.sbuf_tensor(f"wkh{g}", [128, 2048], BF16))
                WVH = QT(nc.sbuf_tensor(f"wvh{g}", [128, 2048], BF16))
                WTFB = QT(nc.sbuf_tensor(f"wtfb{g}", [128, 2048], F32))
                WT2 = QT(nc.sbuf_tensor(f"wt2{g}", [128, 2048], BF16))
                BVS = QT(nc.sbuf_tensor(f"bvs{g}", [128, 256], F32))

                P.em("gpsimd", lambda gp, WKH=WKH, d0=d0: gp.dma_start(
                    WKH[:, :].rearrange("p (cc d) -> p cc d", d=256),
                    d_w["wk"][:, d0:d0 + 256].rearrange("(cc p) d -> p cc d", p=128)),
                    inc="Gwk", inck=16, key=("wkh", g))
                P.em("gpsimd", lambda gp, d0=d0, WVH=WVH: gp.dma_start(
                    WVH[:, :].rearrange("p (cc d) -> p cc d", d=256),
                    d_w["wv"][:, d0:d0 + 256].rearrange("(cc p) d -> p cc d", p=128)),
                    inc="Gwv", inck=16, key=("wvh", g))
                P.em("gpsimd", lambda gp, d0=d0, WQH=WQH: gp.dma_start(
                    WQH[:, :].rearrange("p (cc d) -> p cc d", d=256),
                    d_w["wq"][:, d0:d0 + 256].rearrange("(cc p) d -> p cc d", p=128)),
                    inc="Gwq", inck=16, key=("wqh", g))
                P.em("vector", lambda v, AUGK=AUGK: v.memset(AUGK[64:128, :], 0.0),
                     inc="VE", key=("akz", g))
                P.em("vector", lambda v, AUGQ=AUGQ: v.memset(AUGQ[64:128, :], 0.0))
                P.em("vector", lambda v, VAUG=VAUG: v.memset(VAUG[:, :], 1.0),
                     inc="VE", key=("vz", g))

                for side, (wsrc, WRdst, OFFdst, bcol) in enumerate(
                        ((d_w["wk"], WRK, OFFK, BKTB), (d_w["wq"], WRQ, OFFQ, BQTB))):
                    P.em("sync", lambda sy, wsrc=wsrc, d0=d0, WTFB=WTFB: sy.dma_start(
                        WTFB[:, :].rearrange("p (c n) -> p c n", n=256),
                        wsrc[:, d0:d0 + 256].rearrange("(c p) n -> p c n", p=128)),
                        waits=[("wtt", g, 0, 7, 1)] if side == 1 else [],
                        inc=f"Cwtfb{side}", inck=16, key=("wtl2", g, side))
                    for cc in range(8):
                        kl = side * 8 + cc
                        for db in range(2):
                            kser = kl * 2 + db
                            pt, po = TRB[kser % 6]
                            w2 = [("wtl2", g, side)] if cc == 0 and db == 0 else []
                            if kser >= 6:
                                pk = kser - 6
                                w2.append(("wte", g, pk // 16, (pk % 16) // 2, pk % 2))
                            P.em("tensor", lambda t, cc=cc, db=db, pt=pt, po=po,
                                 WTFB=WTFB: t.transpose(
                                    pt[:, po:po + 128],
                                    WTFB[:, cc * 256 + db * 128:cc * 256 + (db + 1) * 128],
                                    IDENTF[:, :]),
                                waits=w2, inc="TE", key=("wtt", g, side, cc, db))
                            P.em("vector", lambda v, cc=cc, db=db, pt=pt, po=po,
                                 WT2=WT2: v.tensor_copy(
                                    WT2[:, db * 1024 + cc * 128: db * 1024 + (cc + 1) * 128],
                                    pt[:, po:po + 128]),
                                waits=(("wtt", g, side, cc, db),), inc="VE",
                                key=("wte", g, side, cc, db))
                    for cc in range(8):
                        pb = (cc % 2) * 512
                        for hl in range(HPQ):
                            w3 = []
                            if cc == 0 and hl == 0:
                                w3 = [("wte", g, side, 7, 1), "v_bq"]
                            elif hl == 0 and cc >= 2:
                                w3 = [("wre", g, side, cc - 2)]
                            P.em("tensor", lambda t, cc=cc, hl=hl, pb=pb, WT2=WT2: t.matmul(
                                PSD[0:128, pb + hl * 8:pb + (hl + 1) * 8],
                                WT2[(hl % 2) * 64:(hl % 2) * 64 + 64,
                                    (hl // 2) * 1024 + cc * 128:(hl // 2) * 1024 + (cc + 1) * 128],
                                RS[(hl % 2) * 64:(hl % 2) * 64 + 64, 0:8],
                                start=True, stop=True),
                                waits=w3, inc="TE", key=("wrm", g, side, cc, hl))
                        P.em("vector", lambda v, cc=cc, pb=pb, WRdst=WRdst: v.tensor_copy(
                            WRdst[:, cc * 32:(cc + 1) * 32], PSD[0:128, pb:pb + 32]),
                            waits=(("wrm", g, side, cc, HPQ - 1),), inc="VE",
                            key=("wre", g, side, cc))
                    for cc in range(8):
                        P.em("tensor", lambda t, cc=cc, WRdst=WRdst, bcol=bcol: t.matmul(
                            PSD[0:32, 40:41], WRdst[:, cc * 32:(cc + 1) * 32],
                            bcol[:, cc:cc + 1], start=(cc == 0), stop=(cc == 7)),
                            waits=(("wre", g, side, 7),) if cc == 0 else (),
                            inc="TE" if cc == 7 else None, key=("offm", g, side))
                    P.em("vector", lambda v, OFFdst=OFFdst: v.tensor_copy(
                        OFFdst[:, :], PSD[0:32, 40:41]),
                        waits=(("offm", g, side),), inc="VE", key=("off", g, side))

                KSL = [(PSA, 0), (PSB, 0), (PSD, 0), (PSD, 512)]
                for jt in range(8):
                    for p in range(2):
                        kk = jt * 2 + p
                        kt, ko = KSL[kk % 4]
                        w = [("wkh", g), ("wte", g, 1, 7, 1), ("off", g, 1)] if kk == 0 else []
                        if kk >= 4:
                            w.append(("kev", g, kk - 4, 1))
                        for cc in range(8):
                            P.em("tensor", lambda t, cc=cc, p=p, jt=jt, kk=kk, kt=kt, ko=ko,
                                 WKH=WKH: t.matmul(
                                    kt[:, ko:ko + 512],
                                    WKH[:, cc * 256 + p * 128:cc * 256 + (p + 1) * 128],
                                    XNT[cc][:, jt * 512:(jt + 1) * 512],
                                    start=(cc == 0), stop=(cc == 7)),
                                waits=w if cc == 0 else (), inc="TE" if cc == 7 else None,
                                key=("kpm", g, kk))
                        for eo in range(2):
                            hl = 2 * p + eo
                            P.em("vector", lambda v, hl=hl, eo=eo, p=p, jt=jt, kk=kk,
                                 kt=kt, ko=ko, g=g, AUGK=AUGK: v.tensor_scalar(
                                     AUGK[0:64, hl * SEQ + jt * 512: hl * SEQ + (jt + 1) * 512],
                                     kt[eo * 64:eo * 64 + 64, ko:ko + 512],
                                     BKTF[eo * 64:eo * 64 + 64, 2 * g + p:2 * g + p + 1],
                                     F_EIGHTH, ALU.add, ALU.mult),
                                 waits=(("kpm", g, kk), ("akz", g)) if eo == 0 else (("kpm", g, kk),),
                                 inc="VE", key=("kev", g, kk, eo))

                P.em("tensor", lambda t, d0=d0: t.matmul(
                    PSD[0:128, 512:768], ONESB[0:1, 0:128],
                    BVROW[0:1, d0:d0 + 256], start=True, stop=True),
                    waits=("ld_bv", ("wre", g, 1, 7), ("kev", g, 15, 1)),
                    inc="TE", key=("bvm", g))
                P.em("vector", lambda v, BVS=BVS: v.tensor_copy(BVS[:, :], PSD[0:128, 512:768]),
                     waits=(("bvm", g),), inc="VE", key=("bve", g))
                VSL = [(PSA, 512), (PSB, 512), (PSC, 0), (PSC, 512)]
                for vt in range(32):
                    vq, vo = VSL[vt % 4]
                    w = [("wvh", g), ("bve", g), ("vz", g)] if vt == 0 else []
                    if vt >= 4:
                        w.append(("vev", g, vt - 4))
                    for cc in range(8):
                        P.em("tensor", lambda t, cc=cc, vt=vt, vq=vq, vo=vo, WVH=WVH: t.matmul(
                            vq[:, vo:vo + 256],
                            XNT[cc][:, vt * 128:(vt + 1) * 128],
                            WVH[:, cc * 256:(cc + 1) * 256],
                            start=(cc == 0), stop=(cc == 7)),
                            waits=w if cc == 0 else (), inc="TE" if cc == 7 else None,
                            key=("vpm", g, vt))
                    P.em("vector", lambda v, vt=vt, vq=vq, vo=vo, VAUG=VAUG, BVS=BVS: v.tensor_tensor(
                        VAUG[:, vt * 260:(vt + 1) * 260].rearrange(
                            "p (h d) -> p h d", d=65)[:, :, 0:64],
                        vq[:, vo:vo + 256].rearrange("p (h d) -> p h d", d=64),
                        BVS[:, :].rearrange("p (h d) -> p h d", d=64), ALU.add),
                        waits=(("vpm", g, vt),), inc="VE", key=("vev", g, vt))

                def bits_block(tag, idx, rhs_fn, WRsrc, OFFsrc, dst, dstcol,
                               logit_extra, prev_bitm, bits_extra):
                    for cc in range(8):
                        w = []
                        if cc == 0:
                            w = [("off", g, 1)] + list(logit_extra)
                        P.em("tensor", lambda t, cc=cc, rhs=rhs_fn(cc), WRsrc=WRsrc: t.matmul(
                            PSD[0:32, 0:512], WRsrc[:, cc * 32:(cc + 1) * 32], rhs,
                            start=(cc == 0), stop=(cc == 7)),
                            waits=w, inc="TE" if cc == 7 else None, key=(tag, "lgm", g, idx))
                    P.em("vector", lambda v, OFFsrc=OFFsrc: v.tensor_scalar(
                        LSB[0:32, :], PSD[0:32, 0:512], OFFsrc[:, :], None, ALU.add),
                        waits=((tag, "lgm", g, idx), "v_c0"), inc="VE", key=(tag, "lg", g, idx))
                    bufs = (LSB, LA, LB, LC)
                    for ri in range(3):
                        P.em("tensor", lambda t, ri=ri, b=bufs[ri]: t.matmul(
                            PSD[0:96, 512:1024], PERMb[0:96, ri * 96:(ri + 1) * 96],
                            b[0:96, :], start=True, stop=True),
                            waits=((tag, "lg", g, idx),) if ri == 0 else ((tag, "bfm", g, idx, ri - 1),),
                            inc="TE", key=(tag, "bfp", g, idx, ri))
                        P.em("vector", lambda v, ri=ri, b=bufs[ri], b2=bufs[ri + 1]: v.tensor_max(
                            b2[0:32, :], b[0:32, :], PSD[0:32, 512:1024]),
                            waits=((tag, "bfp", g, idx, ri),), inc="VE", key=(tag, "bfm", g, idx, ri))
                    we = [(tag, "bfm", g, idx, 2)]
                    if prev_bitm is not None:
                        we.append(prev_bitm)
                    P.em("vector", lambda v: v.tensor_tensor(
                        ONEHOT[0:32, :], LSB[0:32, :], LC[0:32, :], ALU.is_equal),
                        waits=we, inc="VE", key=(tag, "oh", g, idx))
                    for hl in range(HPQ):
                        w4 = [(tag, "oh", g, idx)]
                        if hl >= 2:
                            w4.append((tag, "bitcp", g, idx, hl - 2))
                        elif bits_extra:
                            w4 += list(bits_extra)
                        P.em("tensor", lambda t, hl=hl: t.matmul(
                            PSC[0:3, (hl % 2) * 512:(hl % 2) * 512 + 512],
                            CBb[0:96, 3 * hl:3 * hl + 3], ONEHOT[0:96, :],
                            start=True, stop=True),
                            waits=w4, inc="TE", key=(tag, "bitm", g, idx, hl))
                        P.em("vector", lambda v, hl=hl, dst=dst, c0=dstcol(hl): v.tensor_copy(
                            dst[64:67, c0:c0 + 512], PSC[0:3, (hl % 2) * 512:(hl % 2) * 512 + 512]),
                            waits=((tag, "bitm", g, idx, hl),), inc="VE",
                            key=(tag, "bitcp", g, idx, hl))

                def kb_lgm(jt):
                    for cc in range(8):
                        w = []
                        if cc == 0:
                            w = [("off", g, 1)] if jt == 0 else [("kb", "lg", g, jt - 1)]
                        P.em("tensor", lambda t, cc=cc, jt=jt, WRK=WRK: t.matmul(
                            PSD[0:32, 0:512], WRK[:, cc * 32:(cc + 1) * 32],
                            XNT[cc][:, jt * 512:(jt + 1) * 512],
                            start=(cc == 0), stop=(cc == 7)),
                            waits=w, inc="TE" if cc == 7 else None, key=("kb", "lgm", g, jt))

                def kb_lg(jt):
                    w = [("kb", "lgm", g, jt), "v_c0"]
                    LB_ = LSB if jt % 2 == 0 else LSB2
                    P.em("vector", lambda v, LB_=LB_: v.tensor_scalar(
                        LB_[0:32, :], PSD[0:32, 0:512], OFFK[:, :], None, ALU.add),
                        waits=w, inc="VE", key=("kb", "lg", g, jt))

                def kb_round(jt, ri):
                    LB_ = LSB if jt % 2 == 0 else LSB2
                    bufs = (LB_, LA, LB, LC)
                    P.em("tensor", lambda t, ri=ri, b=bufs[ri]: t.matmul(
                        PSD[0:96, 512:1024], PERMb[0:96, ri * 96:(ri + 1) * 96],
                        b[0:96, :], start=True, stop=True),
                        waits=(("kb", "lg", g, jt),) if ri == 0 else
                        (("kb", "bfm", g, jt, ri - 1),),
                        inc="TE", key=("kb", "bfp", g, jt, ri))
                    P.em("vector", lambda v, ri=ri, b=bufs[ri], b2=bufs[ri + 1]: v.tensor_max(
                        b2[0:32, :], b[0:32, :], PSD[0:32, 512:1024]),
                        waits=(("kb", "bfp", g, jt, ri),), inc="VE",
                        key=("kb", "bfm", g, jt, ri))

                def kb_tail(jt):
                    LB_ = LSB if jt % 2 == 0 else LSB2
                    we = [("kb", "bfm", g, jt, 2)]
                    if jt > 0:
                        we.append(("kb", "bitm", g, jt - 1, HPQ - 1))
                    P.em("vector", lambda v, LB_=LB_: v.tensor_tensor(
                        ONEHOT[0:32, :], LB_[0:32, :], LC[0:32, :], ALU.is_equal),
                        waits=we, inc="VE", key=("kb", "oh", g, jt))
                    for hl in range(HPQ):
                        w4 = [("kb", "oh", g, jt)]
                        if hl >= 2:
                            w4.append(("kb", "bitcp", g, jt, hl - 2))
                        elif jt > 0:
                            w4 += [("kb", "bitcp", g, jt - 1, 2), ("kb", "bitcp", g, jt - 1, 3)]
                        P.em("tensor", lambda t, hl=hl: t.matmul(
                            PSC[0:3, (hl % 2) * 512:(hl % 2) * 512 + 512],
                            CBb[0:96, 3 * hl:3 * hl + 3], ONEHOT[0:96, :],
                            start=True, stop=True),
                            waits=w4, inc="TE", key=("kb", "bitm", g, jt, hl))
                        P.em("vector", lambda v, hl=hl, jt=jt, AUGK=AUGK: v.tensor_copy(
                            AUGK[64:67, hl * SEQ + jt * 512:hl * SEQ + jt * 512 + 512],
                            PSC[0:3, (hl % 2) * 512:(hl % 2) * 512 + 512]),
                            waits=(("kb", "bitm", g, jt, hl),), inc="VE",
                            key=("kb", "bitcp", g, jt, hl))

                for jt in range(9):
                    if jt < 8:
                        kb_lgm(jt)
                    if jt >= 1:
                        for ri in range(3):
                            kb_round(jt - 1, ri)
                        kb_tail(jt - 1)
                    if jt < 8:
                        kb_lg(jt)

                for it in range(8):
                    for p in range(2):
                        qq = it * 2 + p
                        w = []
                        if qq == 0:
                            w = [("wqh", g), ("kev", g, 15, 1)]
                        if it > 0:
                            w.append(("exp", g, it - 1, 63 if p == 0 else 61))
                        for cc in range(8):
                            P.em("tensor", lambda t, cc=cc, p=p, it=it, qq=qq, WQH=WQH: t.matmul(
                                SCPS[qq % 2][:, 0:512],
                                WQH[:, cc * 256 + p * 128:cc * 256 + (p + 1) * 128],
                                XNT[cc][:, it * 512:(it + 1) * 512],
                                start=(cc == 0), stop=(cc == 7)),
                                waits=w if cc == 0 else (),
                                inc="TE" if cc == 7 else None, key=("qpm", g, qq))
                        for eo in range(2):
                            hl = 2 * p + eo
                            P.em("vector", lambda v, hl=hl, eo=eo, qq=qq, p=p, g=g, AUGQ=AUGQ:
                                 v.tensor_scalar(
                                     AUGQ[0:64, hl * 512:(hl + 1) * 512],
                                     SCPS[qq % 2][eo * 64:eo * 64 + 64, 0:512],
                                     BQTF[eo * 64:eo * 64 + 64, 2 * g + p:2 * g + p + 1],
                                     None, ALU.add),
                                 waits=(("qpm", g, qq),), inc="VE", key=("qev", g, qq, eo))
                    if it == 0:
                        lex = [("kb", "lg", g, 7)]
                        pbm = ("kb", "bitm", g, 7, HPQ - 1)
                        bex = [("kb", "bitcp", g, 7, 2), ("kb", "bitcp", g, 7, 3)]
                    else:
                        lex = [("rc", g, it - 1, HPQ - 1), ("exp", g, it - 1, 62)]
                        pbm = ("qb", "bitm", g, it - 1, HPQ - 1)
                        bex = [("cmul", g, it - 1, HPQ - 1)]
                    bits_block("qb", it,
                               lambda cc, it=it: XNT[cc][:, it * 512:(it + 1) * 512],
                               WRQ, OFFQ, AUGQ, lambda hl: hl * 512,
                               logit_extra=lex, prev_bitm=pbm, bits_extra=bex)
                    LAG = 3
                    SLOT = (PSA, PSB, PSD)

                    def emit_scores(lin, it=it, g=g):
                        hl, jj = divmod(lin, 16)
                        w5 = []
                        if lin == 0:
                            w5 = [("qev", g, it * 2 + 1, 1),
                                  ("qb", "bitcp", g, it, HPQ - 1),
                                  ("kb", "bitcp", g, 7, HPQ - 1),
                                  ("kev", g, 15, 1), ("vev", g, 31)]
                        if lin >= LAG:
                            w5.append(("exp", g, it, lin - LAG))
                        elif it > 0:
                            w5.append(("exp", g, it - 1, 61 + lin))
                        if lin == 2:
                            w5.append(("qb", "bfm", g, it, 2))
                        for sub in range(2):
                            j = 2 * jj + sub
                            P.em("tensor", lambda t, hl=hl, j=j, lin=lin, sub=sub,
                                 AUGK=AUGK, AUGQ=AUGQ: t.matmul(
                                     SLOT[lin % 3][:, sub * 512:sub * 512 + 512],
                                     AUGK[0:96, hl * SEQ + j * 128: hl * SEQ + (j + 1) * 128],
                                     AUGQ[0:96, hl * 512:(hl + 1) * 512],
                                     start=True, stop=True),
                                 waits=w5 if sub == 0 else (),
                                 inc="TE" if sub == 1 else None, key=("scm", g, it, lin))
                        we2 = [("scm", g, it, lin)]
                        if lin >= 8:
                            we2.append(("pvm", g, it, lin - 8))
                        elif it > 0:
                            we2.append(("pvm", g, it - 1, 56 + lin))
                        P.em("scalar", lambda s, lin=lin: s.activation(
                            EXPS[lin % 8][:, :], SLOT[lin % 3][:, :], AF.Exp, bias=EXP_BIAS),
                            waits=we2, inc="SE", key=("exp", g, it, lin))

                    def emit_pv(plin, it=it, g=g):
                        hl, jjp = divmod(plin, 16)
                        for sub in range(2):
                            j = 2 * jjp + sub
                            w6 = [("exp", g, it, plin)] if sub == 0 else []
                            if jjp == 0 and sub == 0 and (it > 0 or hl >= 2):
                                pit, phl = (it, hl - 2) if hl >= 2 else (it - 1, hl + 2)
                                w6.append(("cmul", g, pit, phl))
                            P.em("tensor", lambda t, hl=hl, j=j, plin=plin, sub=sub,
                                 VAUG=VAUG: t.matmul(
                                     PSC[0:65, (hl % 2) * 512:(hl % 2) * 512 + 512],
                                     VAUG[:, (j * HPQ + hl) * 65:(j * HPQ + hl) * 65 + 65],
                                     EXPS[plin % 8][:, sub * 512:sub * 512 + 512],
                                     start=(jjp == 0 and sub == 0), stop=(jjp == 15 and sub == 1)),
                                 waits=w6, inc="TE" if sub == 1 else None,
                                 key=("pvm", g, it, plin))
                        if jjp == 15:
                            emit_tail(hl)

                    def emit_tail(hl, it=it, g=g):
                        par = hl % 2
                        wdn = [("pvm", g, it, hl * 16 + 15)]
                        if debug and (g, it, hl) == (0, 7, 1):
                            wdn.append("dbg_dn_st")
                        P.em("vector", lambda v, hl=hl, par=par: v.tensor_copy(
                            DNBF[par][0:1, :], PSC[64:65, (hl % 2) * 512:(hl % 2) * 512 + 512]),
                            waits=wdn, inc="VE", key=("dn", g, it, hl))
                        if debug and (g, it, hl) == (0, 7, 0):
                            P.em("sync", lambda sy: sy.dma_start(d_ddn, DNBF[0][:, :]),
                                 waits=(("dn", 0, 7, 0),), inc="Cdbg", inck=16, key="dbg_dn_st")
                            P.em("vector", lambda v: v.tensor_copy(RCF[0][:, :], PSC[0:64, 0:512]),
                                 waits=(("pvm", 0, 7, 15),), inc="VE", key="dbg_pv_ev")
                            P.em("sync", lambda sy: sy.dma_start(d_dpv, RCF[0][:, :]),
                                 waits=("dbg_pv_ev",), inc="Cdbg", inck=16, key="dbg_pv_st")
                        w7 = [("dn", g, it, hl)]
                        P.em("tensor", lambda t, hl=hl, par=par: t.matmul(
                            PSC[64:128, (hl % 2) * 512:(hl % 2) * 512 + 512],
                            ONESB[0:1, 0:64], DNBF[par][0:1, :],
                            start=True, stop=True),
                            waits=w7, inc="TE", key=("bcm", g, it, hl))
                        wbe = [("bcm", g, it, hl)]
                        if it * 4 + hl >= 2:
                            k3 = it * 4 + hl - 2
                            wbe.append(("rc", g, k3 // 4, k3 % 4))
                        P.em("vector", lambda v, hl=hl, par=par: v.tensor_copy(
                            BCS[par][:, :], PSC[64:128, (hl % 2) * 512:(hl % 2) * 512 + 512]),
                            waits=wbe, inc="VE", key=("bce", g, it, hl))

                    def emit_tail2(hl, it=it, g=g):
                        par = hl % 2
                        HG = 4 * g + hl
                        pr = (hl // 2) % 2
                        wrl = [("bce", g, it, hl)]
                        if debug and (g, it, hl) == (0, 7, 0):
                            wrl.append("dbg_pv_st")
                        P.em("scalar", lambda s, par=par: s.activation(
                            RCF[par][:, :], BCS[par][:, :], AF.Ln),
                            waits=wrl, inc="SE", key=("rcl", g, it, hl))
                        P.em("scalar", lambda s, par=par: s.activation(
                            BCS[par][:, :], RCF[par][:, :], AF.Exp, scale=np.float32(-1.0)),
                            waits=(("rcl", g, it, hl),), inc="SE", key=("rc", g, it, hl))
                        if debug and (g, it, hl) == (0, 7, 0):
                            P.em("sync", lambda sy: sy.dma_start(d_dbcs, BCS[0][:, :]),
                                 waits=(("rc", 0, 7, 0),), inc="Cdbg", inck=16, key="dbg_bcs_st")
                        w8 = [("rc", g, it, hl)]
                        if it * 4 + hl >= 4:
                            k2 = it * 4 + hl - 4
                            w8.append(("cst", g, k2 // 4, (k2 % 4) // 2))
                        P.em("vector", lambda v, hl=hl, pr=pr, par=par: v.tensor_tensor(
                            CATS[pr][(hl % 2) * 64:(hl % 2) * 64 + 64, :],
                            PSC[0:64, (hl % 2) * 512:(hl % 2) * 512 + 512],
                            BCS[par][0:64, :], ALU.mult),
                            waits=w8, inc="VE", key=("cmul", g, it, hl))
                        if hl % 2 == 1:
                            ch = HG // 2
                            P.em("sync", lambda sy, ch=ch, it=it, pr=pr: sy.dma_start(
                                d_cat[ch * 128:(ch + 1) * 128, it * 512:(it + 1) * 512],
                                CATS[pr][:, :]),
                                waits=(("cmul", g, it, hl),), inc=f"Ccst{pr}", inck=16,
                                key=("cst", g, it, hl // 2))

                    pend = []
                    for lin in range(64 + LAG + 4):
                        if lin < 64:
                            emit_scores(lin)
                        if LAG <= lin < 64 + LAG:
                            emit_pv(lin - LAG)
                        if pend and pend[0][0] <= lin:
                            emit_tail2(pend.pop(0)[1])
                        if lin >= LAG and (lin - LAG) % 16 == 15:
                            pend.append((lin + 4, (lin - LAG) // 16))
                    while pend:
                        emit_tail2(pend.pop(0)[1])
            if debug and g == 0:
                P.em("sync", lambda sy, AUGK=AUGK: sy.dma_start(d_daugk, AUGK[:, :]),
                     inc="ST", inck=16)
                P.em("sync", lambda sy, VAUG=VAUG: sy.dma_start(d_dvaug, VAUG[:, :]),
                     inc="ST", inck=16)
                P.em("sync", lambda sy, AUGQ=AUGQ: sy.dma_start(d_daugq, AUGQ[:, :]),
                     inc="ST", inck=16)
            P.barrier()
        xnt_scope.close()

        pre = ExitStack()
        with pre:
            WGI = [pre.enter_context(nc.sbuf_tensor(f"wgi{i}", [128, 1024], BF16)) for i in range(4)]
            TST = [pre.enter_context(nc.sbuf_tensor(f"tst{i}", [128, 2048], BF16)) for i in range(2)]
            WDI = [pre.enter_context(nc.sbuf_tensor(f"wdi{i}", [128, 3072], BF16)) for i in range(4)]
            TSD = [pre.enter_context(nc.sbuf_tensor(f"tsd{i}", [128, 128], BF16)) for i in range(6)]

            TSL = [(PSC, 0), (PSA, 0), (PSB, 0), (PSD, 0),
                   (PSC, 512), (PSA, 512), (PSB, 512), (PSD, 512)]
            tser = [0]
            kkd = [0]

            def wgu_unit(li):
                hc, half = divmod(li, 2)
                wsrc = d_wg if half == 0 else d_wu
                w = [("pgt", li - 4, 7)] if li >= 4 else []
                P.em("gpsimd", lambda gp, hc=hc, li=li, wsrc=wsrc: gp.dma_start(
                    WGI[li % 4][:, :], wsrc[hc * 128:(hc + 1) * 128, :]),
                    waits=w, inc=f"Ggi{li % 4}", inck=16, key=("pgl", li))
                for cb in range(8):
                    ts = tser[0]
                    pt, po = TSL[ts % 8]
                    w2 = [("pgl", li)] if cb == 0 else []
                    if li == 0 and cb == 0:
                        w2.append("ld_idb")
                    if ts >= 8:
                        w2.append(("tev", ts - 8))
                    P.em("tensor", lambda t, li=li, cb=cb, pt=pt, po=po: t.matmul(
                        pt[:, po:po + 128],
                        WGI[li % 4][:, cb * 128:(cb + 1) * 128], IDENTB[:, :],
                        start=True, stop=True),
                        waits=w2, inc="TE", key=("pgt", li, cb))
                    w3 = [("pgt", li, cb)]
                    if cb == 0 and half == 0 and hc >= 2:
                        w3.append(("pgs", hc - 2))
                    P.em("vector", lambda v, half=half, cb=cb, li=li, pt=pt, po=po: v.tensor_copy(
                        TST[(li // 2) % 2][:, half * 1024 + cb * 128: half * 1024 + (cb + 1) * 128],
                        pt[:, po:po + 128]),
                        waits=w3, inc="VE", key=("tev", ts))
                    tser[0] += 1
                if half == 1:
                    P.em("sync", lambda sy, hc=hc, ts=tser[0] - 1: sy.dma_start(
                        d_wgu[hc * 128:(hc + 1) * 128, :], TST[hc % 2][:, :]),
                        waits=(("tev", ts),), inc=f"Ctst{hc % 2}", inck=16, key=("pgs", hc))

            def wd_unit(rb):
                w = [("pdt", rb - 4, 23)] if rb >= 4 else []
                P.em("gpsimd", lambda gp, rb=rb: gp.dma_start(
                    WDI[rb % 4][:, :], d_wd[rb * 128:(rb + 1) * 128, :]),
                    waits=w, inc=f"Gdi{rb % 4}", inck=16, key=("pdl", rb))
                for hb in range(24):
                    ts = tser[0]
                    kk = kkd[0]
                    pt, po = TSL[ts % 8]
                    w2 = [("pdl", rb)] if hb == 0 else []
                    if ts >= 8:
                        w2.append(("tev", ts - 8))
                    P.em("tensor", lambda t, rb=rb, hb=hb, pt=pt, po=po: t.matmul(
                        pt[:, po:po + 128],
                        WDI[rb % 4][:, hb * 128:(hb + 1) * 128], IDENTB[:, :],
                        start=True, stop=True),
                        waits=w2, inc="TE", key=("pdt", rb, hb))
                    w3 = [("pdt", rb, hb)]
                    if kk >= 6:
                        w3.append(("pds", kk - 6))
                    P.em("vector", lambda v, hb=hb, kk=kk, pt=pt, po=po: v.tensor_copy(
                        TSD[kk % 6][:, :], pt[:, po:po + 128]),
                        waits=w3, inc="VE", key=("tev", ts))
                    P.em("sync", lambda sy, rb=rb, hb=hb, kk=kk: sy.dma_start(
                        d_wdt[hb * 128:(hb + 1) * 128, rb * 128:(rb + 1) * 128],
                        TSD[kk % 6][:, :]),
                        waits=(("tev", ts),), inc=f"Ctsd{kk % 6}", inck=16, key=("pds", kk))
                    tser[0] += 1
                    kkd[0] += 1

            for grp in range(8):
                for li in range(grp * 6, grp * 6 + 6):
                    wgu_unit(li)
                wd_unit(grp)
        P.barrier()

        ds = ExitStack()
        with ds:
            DT = ds.enter_context
            XB = [DT(nc.sbuf_tensor("xb0", [128, 4096], F32))]
            WGUC = [DT(nc.sbuf_tensor(f"wguc{i}", [128, 2048], BF16)) for i in range(4)]
            GSB = [DT(nc.sbuf_tensor(f"gsb{i}", [128, 512], F32)) for i in range(2)]
            HTS = DT(nc.sbuf_tensor("hts", [128, 24 * 512], BF16))
            WDTS = DT(nc.sbuf_tensor("wdts", [128, 24 * 1024], BF16))
            WOSB = DT(nc.sbuf_tensor("wosb", [128, 8 * 1024], BF16))
            CTC = [DT(nc.sbuf_tensor("ctc0", [128, 4096], BF16))]
            OSB = [DT(nc.sbuf_tensor("osb0", [128, 1024], F32))]
            DSTAT = DT(nc.sbuf_tensor("dstat_s", [128, 256], F32)) if debug else None

            for q in range(4):
                P.em("sync", lambda sy, q=q: sy.dma_start(
                    WDTS[:, q * 6144:(q + 1) * 6144].rearrange("p (hc e) -> p hc e", e=1024),
                    d_wdt[q * 768 * 1024 // 1024 * 1024 // 1024 * 0 + q * 6 * 128:(q + 1) * 6 * 128, :]
                    .rearrange("(hc p) e -> p hc e", p=128)),
                    inc=f"Cwdt{q}", inck=16, key=("wdl", q))
            P.em("gpsimd", lambda gp: gp.dma_start(
                WOSB[:, :].rearrange("p (ch e) -> p ch e", e=1024),
                d_w["wo"].rearrange("(ch p) e -> p ch e", p=128)),
                inc="Gwo", inck=16, key="wol")

            for ib in range(8):
                P.em("sync", lambda sy, ib=ib: sy.dma_start(
                    CTC[0][:, :].rearrange("p (ch i) -> p ch i", i=512),
                    d_cat[:, ib * 512:(ib + 1) * 512].rearrange("(ch p) i -> p ch i", p=128)),
                    waits=(("fev", ib - 1, 3),) if ib >= 1 else (), inc="Cctc0", inck=16,
                    key=("ctl", ib))
                w = [("fev", ib - 1, s_) for s_ in range(4)] if ib >= 1 else []
                P.em("sync", lambda sy, ib=ib: sy.dma_start(
                    XB[0][:, :].rearrange("p (st d) -> p st d", d=1024),
                    d_x[ib * 512:(ib + 1) * 512, :].rearrange("(st p) d -> p st d", p=128)),
                    waits=w, inc="Cxd0", inck=16, key=("D", "xload", ib))
                for hc in range(24):
                    lk = ib * 24 + hc
                    w = [("gum", lk - 4)] if lk >= 4 else []
                    P.em("gpsimd", lambda gp, hc=hc, lk=lk: gp.dma_start(
                        WGUC[lk % 4][:, :], d_wgu[hc * 128:(hc + 1) * 128, :]),
                        waits=w, inc=f"Ggu{lk % 4}", inck=16, key=("gul", lk))
                    wg_ = [("gul", lk)]
                    if lk >= 2:
                        wg_.append(("hts", lk - 2))
                    for cc in range(8):
                        P.em("tensor", lambda t, hc=hc, cc=cc, lk=lk, ib=ib: t.matmul(
                            SCPS[hc % 2][:, 0:512],
                            WGUC[lk % 4][:, cc * 128:(cc + 1) * 128],
                            XNT[cc][:, ib * 512:(ib + 1) * 512],
                            start=(cc == 0), stop=(cc == 7)),
                            waits=wg_ if cc == 0 else ())
                    for cc in range(8):
                        P.em("tensor", lambda t, hc=hc, cc=cc, lk=lk, ib=ib: t.matmul(
                            SCPS[hc % 2][:, 512:1024],
                            WGUC[lk % 4][:, 1024 + cc * 128:1024 + (cc + 1) * 128],
                            XNT[cc][:, ib * 512:(ib + 1) * 512],
                            start=(cc == 0), stop=(cc == 7)),
                            inc="TE" if cc == 7 else None, key=("gum", lk))
                    ws = [("gum", lk)]
                    if lk >= 2:
                        ws.append(("hts", lk - 2))
                    P.em("scalar", lambda s, hc=hc, lk=lk: s.activation(
                        GSB[hc % 2][:, :], SCPS[hc % 2][:, 0:512], AF.Silu, scale=C127),
                        waits=ws, inc="SE", key=("silu", lk))
                    wh = [("silu", lk)]
                    if ib > 0:
                        wh.append(("fin", ib * 4 - 1))
                    if DBG_XNTB[0] and lk == 24:
                        wh.append("dbg_hts_st")
                    P.em("vector", lambda v, hc=hc, lk=lk: v.scalar_tensor_tensor(
                        HTS[:, hc * 512:(hc + 1) * 512], SCPS[hc % 2][:, 512:1024],
                        C127SQ, GSB[hc % 2][:, :], ALU.mult, ALU.mult),
                        waits=wh, inc="VE", key=("hts", lk))
                if debug and ib == 0:
                    P.em("sync", lambda sy: sy.dma_start(d_dhts, HTS[:, :]),
                         waits=(("hts", 23),), inc="Cdbg", inck=16, key="dbg_hts_st")
                for st in range(4):
                    tt = ib * 4 + st
                    w = [("hts", ib * 24 + 23), ("ctl", ib), "wol"]
                    if tt == 0:
                        w += [("wdl", q_) for q_ in range(4)]
                    if tt >= 1:
                        w.append(("fev", (tt - 1) // 4, (tt - 1) % 4))
                    for eh in range(2):
                        P.em("tensor", lambda t, eh=eh: t.matmul(
                            PSD[:, eh * 512:(eh + 1) * 512], ONESB[0:1, 0:128],
                            BOROW[0:1, eh * 512:(eh + 1) * 512], start=True, stop=False),
                            waits=w if eh == 0 else ())
                    for ch in range(8):
                        for eh in range(2):
                            P.em("tensor", lambda t, ch=ch, eh=eh, ib=ib, st=st: t.matmul(
                                PSD[:, eh * 512:(eh + 1) * 512],
                                CTC[0][:, ch * 512 + st * 128: ch * 512 + (st + 1) * 128],
                                WOSB[:, ch * 1024 + eh * 512: ch * 1024 + (eh + 1) * 512],
                                start=False, stop=False))
                    for hc in range(24):
                        for eh in range(2):
                            last = (hc == 23)
                            P.em("tensor", lambda t, hc=hc, eh=eh, st=st, last=last: t.matmul(
                                PSD[:, eh * 512:(eh + 1) * 512],
                                HTS[:, hc * 512 + st * 128: hc * 512 + (st + 1) * 128],
                                WDTS[:, hc * 1024 + eh * 512: hc * 1024 + (eh + 1) * 512],
                                start=False, stop=last),
                                inc="TE" if (last and eh == 1) else None, key=("fin", tt))
                    wf = [("fin", tt), ("D", "xload", ib)]
                    if tt >= 1:
                        wf.append(("ost", tt - 1))
                    P.em("vector", lambda v, ib=ib, st=st, tt=tt: v.tensor_tensor(
                        OSB[0][:, :], PSD[:, :], XB[0][:, st * 1024:(st + 1) * 1024],
                        ALU.add),
                        waits=wf, inc="VE", key=("fev", ib, st))
                    P.em("sync", lambda sy, tt=tt: sy.dma_start(
                        d_out[tt * 128:(tt + 1) * 128, :], OSB[0][:, :]),
                        waits=(("fev", tt // 4, tt % 4),), inc="Cosb0", inck=16, key=("ost", tt))

            if debug:
                P.em("sync", lambda sy: sy.dma_start(d_dstat, SREG[:, 0:64].rearrange(
                    "p c -> p c")), waits=(("fev", 7, 3),), inc="Cdbg", inck=16)
            fin_waits = tuple(("LIT", s, P.cnt[s]) for s in list(P.cnt)
                              if s.startswith(("Cosb", "Ccst", "Ctst", "Ctsd")))
            P.em("sync", lambda sy: None, waits=fin_waits)

        P.emit(block)
    return nc


_NC_CACHE = None


def kernel(**inputs):
    global _NC_CACHE
    ident, cb, perms = _host_consts()
    x = np.ascontiguousarray(np.asarray(inputs["L0_attn_x"], np.float32))
    base = {
        "wq": np.ascontiguousarray(np.asarray(inputs["L0_attn_Wq"], np.float32)),
        "wk": np.ascontiguousarray(np.asarray(inputs["L0_attn_Wk"], np.float32)),
        "wv": np.ascontiguousarray(np.asarray(inputs["L0_attn_Wv"], np.float32)),
        "wo": np.ascontiguousarray(np.asarray(inputs["L0_attn_Wo"], np.float32)),
        "bq": np.ascontiguousarray(np.asarray(inputs["L0_attn_bq"], np.float32)),
        "bk": np.ascontiguousarray(np.asarray(inputs["L0_attn_bk"], np.float32)),
        "bv": np.ascontiguousarray(np.asarray(inputs["L0_attn_bv"], np.float32)),
        "bo": np.ascontiguousarray(np.asarray(inputs["L0_attn_bo"], np.float32)),
        "r": np.ascontiguousarray(np.asarray(inputs["L0_attn_R"], np.float32)),
        "wg": np.ascontiguousarray(np.asarray(inputs["L0_ffn_w_gate"], np.int32)),
        "wu": np.ascontiguousarray(np.asarray(inputs["L0_ffn_w_up"], np.int32)),
        "wd": np.ascontiguousarray(np.asarray(inputs["L0_ffn_w_down"], np.int32)),
        "identf": ident, "cbc": cb, "permsc": perms,
    }
    if _NC_CACHE is None:
        _NC_CACHE = build_program()
    nc = _NC_CACHE
    in_maps = [dict(base, x=np.ascontiguousarray(x[b])) for b in range(8)]
    res = run_bass_kernel_spmd(nc, in_maps, core_ids=list(range(8)))
    out = np.stack([np.asarray(res.results[b]["out"], np.float32) for b in range(8)], axis=0)
    return out


if __name__ == "__main__":
    nc = build_program()
    print("built ok; instructions:", nc.n_instructions())


# version 25
# speedup vs baseline: 1.1904x; 1.0545x over previous
import numpy as np
import ml_dtypes
import concourse.bass as bass
import concourse.mybir as mybir
from concourse.bass_utils import run_bass_kernel_spmd
from contextlib import ExitStack

F32 = mybir.dt.float32
BF16 = mybir.dt.bfloat16
I32 = mybir.dt.int32
AF = mybir.ActivationFunctionType
ALU = mybir.AluOpType
AX = mybir.AxisListType

SEQ = 4096
D = 1024
DK = 64
NH = 8
HPQ = 4
NQ = 4
HID = 3072
S_BIT = 16.0
EXP_BIAS = np.float32(-3.0 * S_BIT * S_BIT)
C127 = np.float32(1.0 / 127.0)
C127SQ = float(np.float32(1.0 / (127.0 * 127.0)))
RD = 1.0 / D
F_EIGHTH = 0.125

ENGINES = ("sync", "gpsimd", "tensor", "scalar", "vector")


def _host_consts():
    ident = np.eye(128, dtype=np.float32)
    codes = np.array([[S_BIT if (n >> b) & 1 else -S_BIT for b in range(3)]
                      for n in range(NH)], np.float32)
    cb = np.zeros((96, 12), np.float32)
    for h in range(HPQ):
        for n in range(NH):
            cb[8 * h + n, 3 * h: 3 * h + 3] = codes[n]
    perms = np.zeros((96, 288), np.float32)
    for ri, dd in enumerate((1, 2, 4)):
        for h in range(HPQ):
            for n in range(NH):
                perms[8 * h + n, ri * 96 + 8 * h + (n ^ dd)] = 1.0
    return ident, cb, perms


class Prog:

    def __init__(self, nc, es):
        self.nc = nc
        self.es = es
        self.m = {}
        self.cnt = {}
        self.sems = {}
        for k in ("TE", "SE", "VE"):
            self._mk(k)
        self.seq = {e: [] for e in ENGINES}

    def _mk(self, k):
        if k not in self.sems:
            self.sems[k] = self.es.enter_context(self.nc.semaphore(k))
            self.cnt[k] = 0

    def em(self, eng, fn, waits=(), inc=None, inck=1, key=None):
        if inc is not None:
            self._mk(inc)
            self.cnt[inc] += inck
            if key is not None:
                assert key not in self.m, f"duplicate milestone key {key}"
                self.m[key] = (inc, self.cnt[inc])
        self.seq[eng].append((fn, tuple(waits), inc, inck))

    def barrier(self):
        snap = [(s, v) for s, v in self.cnt.items() if v]
        for e in ENGINES:
            self.seq[e].append((None, tuple(("LIT", s, v) for s, v in snap), None, 1))

    def emit(self, block):
        sems, m = self.sems, self.m
        for e in ENGINES:
            ops = self.seq[e]

            def body(eng, ops=ops):
                for fn, waits, inc, inck in ops:
                    for w in waits:
                        if isinstance(w, tuple) and len(w) == 3 and w[0] == "LIT":
                            eng.wait_ge(sems[w[1]], w[2])
                        else:
                            sname, val = m[w]
                            eng.wait_ge(sems[sname], val)
                    if fn is not None:
                        inst = fn(eng)
                        if inc is not None:
                            inst.then_inc(sems[inc], inck)
            getattr(block, e)(body)


def build_program(debug=False):
    nc = bass.Bass("TRN2", target_bir_lowering=False)
    d_x = nc.dram_tensor("x", [SEQ, D], F32, kind="ExternalInput").ap()
    d_w = {n: nc.dram_tensor(n, [D, D], F32, kind="ExternalInput").ap()
           for n in ("wq", "wk", "wv", "wo")}
    d_b = {n: nc.dram_tensor(n, [D], F32, kind="ExternalInput").ap()
           for n in ("bq", "bk", "bv", "bo")}
    d_r = nc.dram_tensor("r", [NH, DK], F32, kind="ExternalInput").ap()
    d_wg = nc.dram_tensor("wg", [HID, D], I32, kind="ExternalInput").ap()
    d_wu = nc.dram_tensor("wu", [HID, D], I32, kind="ExternalInput").ap()
    d_wd = nc.dram_tensor("wd", [D, HID], I32, kind="ExternalInput").ap()
    d_ident = nc.dram_tensor("identf", [128, 128], F32, kind="ExternalInput").ap()
    d_cb = nc.dram_tensor("cbc", [96, 12], F32, kind="ExternalInput").ap()
    d_perms = nc.dram_tensor("permsc", [96, 288], F32, kind="ExternalInput").ap()
    d_out = nc.dram_tensor("out", [SEQ, D], F32, kind="ExternalOutput").ap()
    _scr = "ExternalOutput" if debug else "Internal"
    d_cat = nc.dram_tensor("catscr", [D, SEQ], BF16, kind=_scr).ap()
    d_wgu = nc.dram_tensor("wguscr", [24 * 128, 2048], BF16, kind=_scr).ap()
    d_wdt = nc.dram_tensor("wdtscr", [24 * 128, 1024], BF16, kind=_scr).ap()
    if debug:
        d_dxnt = nc.dram_tensor("dxnt", [1024, SEQ], BF16, kind="ExternalOutput").ap()
        d_daugk = nc.dram_tensor("daugk", [128, HPQ * SEQ], BF16, kind="ExternalOutput").ap()
        d_dvaug = nc.dram_tensor("dvaug", [128, 32 * HPQ * 65], BF16, kind="ExternalOutput").ap()
        d_daugq = nc.dram_tensor("daugq", [128, HPQ * 512], BF16, kind="ExternalOutput").ap()
        d_dexp = nc.dram_tensor("dexp", [128, 1024], BF16, kind="ExternalOutput").ap()
        d_ddn = nc.dram_tensor("ddn", [1, 512], BF16, kind="ExternalOutput").ap()
        d_dbcs = nc.dram_tensor("dbcs", [64, 512], F32, kind="ExternalOutput").ap()
        d_dpv = nc.dram_tensor("dpv", [64, 512], F32, kind="ExternalOutput").ap()
        d_dstat = nc.dram_tensor("dstat", [128, 64], F32, kind="ExternalOutput").ap()
        d_dxntb = nc.dram_tensor("dxntb", [128, 4096], BF16, kind="ExternalOutput").ap()
        d_dhts = nc.dram_tensor("dhts", [128, 12288], BF16, kind="ExternalOutput").ap()

    DBG_XNTB = [debug]
    es = ExitStack()
    with es:
        ET = es.enter_context
        P = Prog(nc, es)

        PSA = ET(nc.psum_tensor("PSA", [128, 1024], F32))
        PSB = ET(nc.psum_tensor("PSB", [128, 1024], F32))
        PSC = ET(nc.psum_tensor("PSC", [128, 1024], F32))
        PSD = ET(nc.psum_tensor("PSD", [128, 1024], F32))
        SCPS = (PSA, PSB)
        TRB = [(PSC, 0), (PSA, 0), (PSB, 0), (PSC, 512), (PSA, 512), (PSB, 512)]

        IDENTF = ET(nc.sbuf_tensor("identf_s", [128, 128], F32))
        IDENTB = ET(nc.sbuf_tensor("identb_s", [128, 128], BF16))
        CBb = ET(nc.sbuf_tensor("cbb", [96, 12], BF16))
        RS = ET(nc.sbuf_tensor("rs", [128, 8], BF16))
        ONESB = ET(nc.sbuf_tensor("onesb", [1, 512], BF16))
        BQTF = ET(nc.sbuf_tensor("bqtf", [128, 8], F32))
        BKTF = ET(nc.sbuf_tensor("bktf", [128, 8], F32))
        BQTB = ET(nc.sbuf_tensor("bqtb", [128, 8], BF16))
        BKTB = ET(nc.sbuf_tensor("bktb", [128, 8], BF16))
        BOROW = ET(nc.sbuf_tensor("borow", [1, D], BF16))
        BVROW = ET(nc.sbuf_tensor("bvrow", [1, D], BF16))
        LNS = [ET(nc.sbuf_tensor(f"lns{i}", [128, 8], F32)) for i in range(2)]
        SREG = ET(nc.sbuf_tensor("sreg", [128, 64], F32))
        P0T = ET(nc.sbuf_tensor("p0t", [8, 128], F32))
        P0R = ET(nc.sbuf_tensor("p0r", [8, 64], F32))

        xnt_scope = ExitStack()
        XS = xnt_scope.enter_context
        XNT = [ET(nc.sbuf_tensor(f"xnt{c}", [128, SEQ], BF16)) for c in range(8)]
        PERMb = XS(nc.sbuf_tensor("permb", [96, 288], F32))
        ONEHOT = XS(nc.sbuf_tensor("onehot", [96, 512], BF16))
        LSB = XS(nc.sbuf_tensor("lsb", [96, 512], F32))
        LSB2 = XS(nc.sbuf_tensor("lsb2", [96, 512], F32))
        LA = XS(nc.sbuf_tensor("la", [96, 512], F32))
        LB = XS(nc.sbuf_tensor("lb", [96, 512], F32))
        LC = XS(nc.sbuf_tensor("lc", [96, 512], F32))
        OFFQ = XS(nc.sbuf_tensor("offq", [32, 1], F32))
        OFFK = XS(nc.sbuf_tensor("offk", [32, 1], F32))
        EXPS = [XS(nc.sbuf_tensor(f"exps{i}", [128, 1024], BF16)) for i in range(10)]
        DNBF = [XS(nc.sbuf_tensor(f"dnbf{i}", [1, 512], BF16)) for i in range(2)]
        BCS = [XS(nc.sbuf_tensor(f"bcs{i}", [64, 512], F32)) for i in range(2)]
        RCF = [XS(nc.sbuf_tensor(f"rcf{i}", [64, 512], F32)) for i in range(2)]
        CATS = [XS(nc.sbuf_tensor(f"cats{i}", [128, 512], BF16)) for i in range(2)]

        a_scope = ExitStack()
        AS = a_scope.enter_context
        XNF = [AS(nc.sbuf_tensor(f"xnf{i}", [128, 1024], F32)) for i in range(2)]
        SQS = AS(nc.sbuf_tensor("sqs", [128, 1024], F32))
        XT = [AS(nc.sbuf_tensor(f"xt{i}", [128, 1024], F32)) for i in range(3)]

        block = ET(nc.Block())

        P.em("sync", lambda sy: sy.dma_start(IDENTF[:, :], d_ident), inc="Cmisc", inck=16, key="ld_id")
        P.em("sync", lambda sy: sy.dma_start(P0T[:, :], d_b["bq"].rearrange("(t p) -> t p", p=128)),
             inc="Cmisc", inck=16, key="ld_bq")
        P.em("sync", lambda sy: sy.dma_start(P0R[:, :], d_r), inc="Cmisc", inck=16, key="ld_r")
        P.em("gpsimd", lambda gp: gp.dma_start(IDENTB[:, :], d_ident), inc="Gmisc", inck=16,
             key="ld_idb")
        P.em("gpsimd", lambda gp: gp.dma_start(CBb[:, :], d_cb), inc="Gmisc", inck=16)
        P.em("gpsimd", lambda gp: gp.dma_start(PERMb[:, :], d_perms), inc="Gmisc", inck=16)
        P.em("gpsimd", lambda gp: gp.dma_start(BOROW[:, :], d_b["bo"].rearrange("(a e) -> a e", a=1)),
             inc="Gmisc", inck=16)
        P.em("gpsimd", lambda gp: gp.dma_start(BVROW[:, :], d_b["bv"].rearrange("(a e) -> a e", a=1)),
             inc="Gmisc", inck=16, key="ld_bv")
        P.em("tensor", lambda t: t.transpose(PSA[0:128, 0:8], P0T[:, :], IDENTF[0:8, 0:8]),
             waits=("ld_r",), inc="TE", key="t_bq")
        P.em("tensor", lambda t: t.transpose(PSA[0:64, 512:520], P0R[:, :], IDENTF[0:8, 0:8]),
             waits=("ld_r",), inc="TE", key="t_r")
        P.em("vector", lambda v: v.tensor_copy(BQTF[:, :], PSA[0:128, 0:8]), waits=("t_bq",))
        P.em("vector", lambda v: v.tensor_copy(BQTB[:, :], PSA[0:128, 0:8]))
        P.em("vector", lambda v: v.tensor_copy(RS[0:64, :], PSA[0:64, 512:520]), waits=("t_r",))
        P.em("vector", lambda v: v.tensor_copy(RS[64:128, :], PSA[0:64, 512:520]),
             inc="VE", key="v_bq")
        P.em("sync", lambda sy: sy.dma_start(P0T[:, :], d_b["bk"].rearrange("(t p) -> t p", p=128)),
             waits=("v_bq",), inc="Cmisc", inck=16, key="ld_bk")
        P.em("tensor", lambda t: t.transpose(PSB[0:128, 0:8], P0T[:, :], IDENTF[0:8, 0:8]),
             waits=("ld_bk",), inc="TE", key="t_bk")
        P.em("vector", lambda v: v.tensor_copy(BKTF[:, :], PSB[0:128, 0:8]), waits=("t_bk",))
        P.em("vector", lambda v: v.tensor_copy(BKTB[:, :], PSB[0:128, 0:8]))
        P.em("vector", lambda v: v.memset(ONESB[:, :], 1.0))
        P.em("vector", lambda v: v.memset(ONEHOT[:, :], 0.0))
        P.em("vector", lambda v: v.memset(LSB[:, :], 0.0))
        P.em("vector", lambda v: v.memset(LSB2[:, :], 0.0))
        P.em("vector", lambda v: v.memset(LA[:, :], 0.0))
        P.em("vector", lambda v: v.memset(LB[:, :], 0.0))
        P.em("vector", lambda v: v.memset(LC[:, :], 0.0), inc="VE", key="v_c0")
        P.barrier()

        def ln_stats(tag, it, par, xsrc):
            L = LNS[par]
            P.em("vector", lambda v, x=xsrc, L=L: v.reduce_sum(L[:, 0:1], x, AX.X),
                waits=((tag, "xload", it),), inc="VE", key=(tag, "red", it))
            P.em("vector", lambda v, L=L: v.tensor_scalar_mul(L[:, 1:2], L[:, 0:1], RD),
                waits=((tag, "red", it),), inc="VE", key=(tag, "mean", it))
            P.em("vector", lambda v, L=L: v.tensor_mul(L[:, 2:3], L[:, 1:2], L[:, 1:2]),
                waits=((tag, "mean", it),), inc="VE", key=(tag, "m2", it))
            P.em("vector", lambda v, x=xsrc: v.tensor_mul(SQS[:, :], x, x),
                 waits=((tag, "xload", it),), inc="VE", key=(tag, "sqm", it))
            P.em("vector", lambda v, L=L: v.reduce_sum(L[:, 6:7], SQS[:, :], AX.X),
                 waits=((tag, "sqm", it),), inc="VE", key=(tag, "sq", it))
            P.em("vector", lambda v, L=L: v.tensor_scalar(
                L[:, 3:4], L[:, 6:7], RD, L[:, 2:3], ALU.mult, ALU.subtract),
                waits=((tag, "sq", it), (tag, "m2", it)), inc="VE", key=(tag, "var", it))
            P.em("scalar", lambda s, L=L: s.activation(
                L[:, 4:5], L[:, 3:4], AF.Ln, bias=np.float32(1e-5)),
                waits=((tag, "var", it),), inc="SE", key=(tag, "lnv", it))
            P.em("scalar", lambda s, L=L: s.activation(
                L[:, 5:6], L[:, 4:5], AF.Exp, scale=np.float32(-0.5)),
                waits=((tag, "lnv", it),), inc="SE", key=(tag, "sqa", it))

        def ln_apply(tag, it, par, xsrc, lpar):
            L = LNS[lpar]
            P.em("vector", lambda v, L=L, x=xsrc, par=par: v.tensor_scalar(
                XNF[par][:, :], x, L[:, 1:2], L[:, 5:6], ALU.subtract, ALU.mult),
                waits=((tag, "sqa", it),), inc="VE", key=(tag, "apply", it))

        def transpose_tile(tag, it, par, dst_fn, k):
            for cc in range(8):
                ts = k + cc
                pt, po = TRB[ts % 6]
                w = [(tag, "apply", it)] if cc == 0 else []
                if ts >= 6:
                    w.append((tag, "xevac", ts - 6))
                P.em("tensor", lambda t, par=par, cc=cc, pt=pt, po=po: t.transpose(
                    pt[:, po:po + 128],
                    XNF[par][:, cc * 128:(cc + 1) * 128], IDENTF[:, :]),
                    waits=w, inc="TE", key=(tag, "trans", ts))
                P.em("vector", lambda v, cc=cc, dst=dst_fn(cc), pt=pt, po=po: v.tensor_copy(
                    dst, pt[:, po:po + 128]),
                    waits=((tag, "trans", ts),), inc="VE", key=(tag, "xevac", ts))

        for it in range(33):
            if it < 32:
                p3 = it % 3
                w = [("A", "apply", it - 3)] if it >= 3 else []
                P.em("sync", lambda sy, it=it, p3=p3: sy.dma_start(
                    XT[p3][:, :], d_x[it * 128:(it + 1) * 128, :]),
                    waits=w, inc=f"CxtA{p3}", inck=16, key=("A", "xload", it))
                ln_stats("A", it, it % 2, XT[p3][:, :])
            if it >= 1:
                jt = it - 1
                ln_apply("A", jt, jt % 2, XT[jt % 3][:, :], jt % 2)
                P.em("vector", lambda v, jt=jt, lp=jt % 2: (
                    v.tensor_copy(SREG[:, jt * 2:jt * 2 + 1], LNS[lp][:, 1:2]),
                    v.tensor_copy(SREG[:, jt * 2 + 1:jt * 2 + 2], LNS[lp][:, 5:6]))[-1],
                    waits=(("A", "apply", jt),), inc="VE", key=("sreg", jt))
                transpose_tile("A", jt, jt % 2,
                               lambda cc, jt=jt: XNT[cc][:, jt * 128:(jt + 1) * 128], jt * 8)
        P.barrier()
        a_scope.close()
        if debug:
            for cc in range(8):
                P.em("sync", lambda sy, cc=cc: sy.dma_start(
                    d_dxnt[cc * 128:(cc + 1) * 128, :], XNT[cc][:, :]), inc="Cdbg", inck=16)
            P.barrier()

        for g in range(NQ):
            d0 = g * 256
            qs = ExitStack()
            with qs:
                QT = qs.enter_context
                AUGK = QT(nc.sbuf_tensor(f"augk{g}", [128, HPQ * SEQ], BF16))
                VAUG = QT(nc.sbuf_tensor(f"vaug{g}", [128, 32 * HPQ * 65], BF16))
                AUGQ = QT(nc.sbuf_tensor(f"augq{g}", [128, HPQ * 512], BF16))
                WQH = QT(nc.sbuf_tensor(f"wqh{g}", [128, 2048], BF16))
                WRQ = QT(nc.sbuf_tensor(f"wrq{g}", [128, 256], BF16))
                WRK = QT(nc.sbuf_tensor(f"wrk{g}", [128, 256], BF16))
                WKH = QT(nc.sbuf_tensor(f"wkh{g}", [128, 2048], BF16))
                WVH = QT(nc.sbuf_tensor(f"wvh{g}", [128, 2048], BF16))
                WTFB = QT(nc.sbuf_tensor(f"wtfb{g}", [128, 2048], F32))
                WT2 = QT(nc.sbuf_tensor(f"wt2{g}", [128, 2048], BF16))
                BVS = QT(nc.sbuf_tensor(f"bvs{g}", [128, 256], F32))

                P.em("gpsimd", lambda gp, WKH=WKH, d0=d0: gp.dma_start(
                    WKH[:, :].rearrange("p (cc d) -> p cc d", d=256),
                    d_w["wk"][:, d0:d0 + 256].rearrange("(cc p) d -> p cc d", p=128)),
                    inc="Gwk", inck=16, key=("wkh", g))
                P.em("gpsimd", lambda gp, d0=d0, WVH=WVH: gp.dma_start(
                    WVH[:, :].rearrange("p (cc d) -> p cc d", d=256),
                    d_w["wv"][:, d0:d0 + 256].rearrange("(cc p) d -> p cc d", p=128)),
                    inc="Gwv", inck=16, key=("wvh", g))
                P.em("gpsimd", lambda gp, d0=d0, WQH=WQH: gp.dma_start(
                    WQH[:, :].rearrange("p (cc d) -> p cc d", d=256),
                    d_w["wq"][:, d0:d0 + 256].rearrange("(cc p) d -> p cc d", p=128)),
                    inc="Gwq", inck=16, key=("wqh", g))
                P.em("vector", lambda v, AUGK=AUGK: v.memset(AUGK[64:128, :], 0.0),
                     inc="VE", key=("akz", g))
                P.em("vector", lambda v, AUGQ=AUGQ: v.memset(AUGQ[64:128, :], 0.0))
                P.em("vector", lambda v, VAUG=VAUG: v.memset(VAUG[:, :], 1.0),
                     inc="VE", key=("vz", g))

                for side, (wsrc, WRdst, OFFdst, bcol) in enumerate(
                        ((d_w["wk"], WRK, OFFK, BKTB), (d_w["wq"], WRQ, OFFQ, BQTB))):
                    P.em("sync", lambda sy, wsrc=wsrc, d0=d0, WTFB=WTFB: sy.dma_start(
                        WTFB[:, :].rearrange("p (c n) -> p c n", n=256),
                        wsrc[:, d0:d0 + 256].rearrange("(c p) n -> p c n", p=128)),
                        waits=[("wtt", g, 0, 7, 1)] if side == 1 else [],
                        inc=f"Cwtfb{side}", inck=16, key=("wtl2", g, side))
                    for cc in range(8):
                        kl = side * 8 + cc
                        for db in range(2):
                            kser = kl * 2 + db
                            pt, po = TRB[kser % 6]
                            w2 = [("wtl2", g, side)] if cc == 0 and db == 0 else []
                            if kser >= 6:
                                pk = kser - 6
                                w2.append(("wte", g, pk // 16, (pk % 16) // 2, pk % 2))
                            P.em("tensor", lambda t, cc=cc, db=db, pt=pt, po=po,
                                 WTFB=WTFB: t.transpose(
                                    pt[:, po:po + 128],
                                    WTFB[:, cc * 256 + db * 128:cc * 256 + (db + 1) * 128],
                                    IDENTF[:, :]),
                                waits=w2, inc="TE", key=("wtt", g, side, cc, db))
                            P.em("vector", lambda v, cc=cc, db=db, pt=pt, po=po,
                                 WT2=WT2: v.tensor_copy(
                                    WT2[:, db * 1024 + cc * 128: db * 1024 + (cc + 1) * 128],
                                    pt[:, po:po + 128]),
                                waits=(("wtt", g, side, cc, db),), inc="VE",
                                key=("wte", g, side, cc, db))
                    for cc in range(8):
                        pb = (cc % 2) * 512
                        for hl in range(HPQ):
                            w3 = []
                            if cc == 0 and hl == 0:
                                w3 = [("wte", g, side, 7, 1), "v_bq"]
                            elif hl == 0 and cc >= 2:
                                w3 = [("wre", g, side, cc - 2)]
                            P.em("tensor", lambda t, cc=cc, hl=hl, pb=pb, WT2=WT2: t.matmul(
                                PSD[0:128, pb + hl * 8:pb + (hl + 1) * 8],
                                WT2[(hl % 2) * 64:(hl % 2) * 64 + 64,
                                    (hl // 2) * 1024 + cc * 128:(hl // 2) * 1024 + (cc + 1) * 128],
                                RS[(hl % 2) * 64:(hl % 2) * 64 + 64, 0:8],
                                start=True, stop=True),
                                waits=w3, inc="TE", key=("wrm", g, side, cc, hl))
                        P.em("vector", lambda v, cc=cc, pb=pb, WRdst=WRdst: v.tensor_copy(
                            WRdst[:, cc * 32:(cc + 1) * 32], PSD[0:128, pb:pb + 32]),
                            waits=(("wrm", g, side, cc, HPQ - 1),), inc="VE",
                            key=("wre", g, side, cc))
                    for cc in range(8):
                        P.em("tensor", lambda t, cc=cc, WRdst=WRdst, bcol=bcol: t.matmul(
                            PSD[0:32, 40:41], WRdst[:, cc * 32:(cc + 1) * 32],
                            bcol[:, cc:cc + 1], start=(cc == 0), stop=(cc == 7)),
                            waits=(("wre", g, side, 7),) if cc == 0 else (),
                            inc="TE" if cc == 7 else None, key=("offm", g, side))
                    P.em("vector", lambda v, OFFdst=OFFdst: v.tensor_copy(
                        OFFdst[:, :], PSD[0:32, 40:41]),
                        waits=(("offm", g, side),), inc="VE", key=("off", g, side))

                KSL = [(PSA, 0), (PSB, 0), (PSD, 0), (PSD, 512)]
                for jt in range(8):
                    for p in range(2):
                        kk = jt * 2 + p
                        kt, ko = KSL[kk % 4]
                        w = [("wkh", g), ("wte", g, 1, 7, 1), ("off", g, 1)] if kk == 0 else []
                        if kk >= 4:
                            w.append(("kev", g, kk - 4, 1))
                        for cc in range(8):
                            P.em("tensor", lambda t, cc=cc, p=p, jt=jt, kk=kk, kt=kt, ko=ko,
                                 WKH=WKH: t.matmul(
                                    kt[:, ko:ko + 512],
                                    WKH[:, cc * 256 + p * 128:cc * 256 + (p + 1) * 128],
                                    XNT[cc][:, jt * 512:(jt + 1) * 512],
                                    start=(cc == 0), stop=(cc == 7)),
                                waits=w if cc == 0 else (), inc="TE" if cc == 7 else None,
                                key=("kpm", g, kk))
                        for eo in range(2):
                            hl = 2 * p + eo
                            P.em("vector", lambda v, hl=hl, eo=eo, p=p, jt=jt, kk=kk,
                                 kt=kt, ko=ko, g=g, AUGK=AUGK: v.tensor_scalar(
                                     AUGK[0:64, hl * SEQ + jt * 512: hl * SEQ + (jt + 1) * 512],
                                     kt[eo * 64:eo * 64 + 64, ko:ko + 512],
                                     BKTF[eo * 64:eo * 64 + 64, 2 * g + p:2 * g + p + 1],
                                     F_EIGHTH, ALU.add, ALU.mult),
                                 waits=(("kpm", g, kk), ("akz", g)) if eo == 0 else (("kpm", g, kk),),
                                 inc="VE", key=("kev", g, kk, eo))

                P.em("tensor", lambda t, d0=d0: t.matmul(
                    PSD[0:128, 512:768], ONESB[0:1, 0:128],
                    BVROW[0:1, d0:d0 + 256], start=True, stop=True),
                    waits=("ld_bv", ("wre", g, 1, 7), ("kev", g, 15, 1)),
                    inc="TE", key=("bvm", g))
                P.em("vector", lambda v, BVS=BVS: v.tensor_copy(BVS[:, :], PSD[0:128, 512:768]),
                     waits=(("bvm", g),), inc="VE", key=("bve", g))
                VSL = [(PSA, 512), (PSB, 512), (PSC, 0), (PSC, 512)]
                for vt in range(32):
                    vq, vo = VSL[vt % 4]
                    w = [("wvh", g), ("bve", g), ("vz", g)] if vt == 0 else []
                    if vt >= 4:
                        w.append(("vev", g, vt - 4))
                    for cc in range(8):
                        P.em("tensor", lambda t, cc=cc, vt=vt, vq=vq, vo=vo, WVH=WVH: t.matmul(
                            vq[:, vo:vo + 256],
                            XNT[cc][:, vt * 128:(vt + 1) * 128],
                            WVH[:, cc * 256:(cc + 1) * 256],
                            start=(cc == 0), stop=(cc == 7)),
                            waits=w if cc == 0 else (), inc="TE" if cc == 7 else None,
                            key=("vpm", g, vt))
                    P.em("vector", lambda v, vt=vt, vq=vq, vo=vo, VAUG=VAUG, BVS=BVS: v.tensor_tensor(
                        VAUG[:, vt * 260:(vt + 1) * 260].rearrange(
                            "p (h d) -> p h d", d=65)[:, :, 0:64],
                        vq[:, vo:vo + 256].rearrange("p (h d) -> p h d", d=64),
                        BVS[:, :].rearrange("p (h d) -> p h d", d=64), ALU.add),
                        waits=(("vpm", g, vt),), inc="VE", key=("vev", g, vt))

                def bits_block(tag, idx, rhs_fn, WRsrc, OFFsrc, dst, dstcol,
                               logit_extra, prev_bitm, bits_extra):
                    for cc in range(8):
                        w = []
                        if cc == 0:
                            w = [("off", g, 1)] + list(logit_extra)
                        P.em("tensor", lambda t, cc=cc, rhs=rhs_fn(cc), WRsrc=WRsrc: t.matmul(
                            PSD[0:32, 0:512], WRsrc[:, cc * 32:(cc + 1) * 32], rhs,
                            start=(cc == 0), stop=(cc == 7)),
                            waits=w, inc="TE" if cc == 7 else None, key=(tag, "lgm", g, idx))
                    P.em("vector", lambda v, OFFsrc=OFFsrc: v.tensor_scalar(
                        LSB[0:32, :], PSD[0:32, 0:512], OFFsrc[:, :], None, ALU.add),
                        waits=((tag, "lgm", g, idx), "v_c0"), inc="VE", key=(tag, "lg", g, idx))
                    bufs = (LSB, LA, LB, LC)
                    for ri in range(3):
                        P.em("tensor", lambda t, ri=ri, b=bufs[ri]: t.matmul(
                            PSD[0:96, 512:1024], PERMb[0:96, ri * 96:(ri + 1) * 96],
                            b[0:96, :], start=True, stop=True),
                            waits=((tag, "lg", g, idx),) if ri == 0 else ((tag, "bfm", g, idx, ri - 1),),
                            inc="TE", key=(tag, "bfp", g, idx, ri))
                        P.em("vector", lambda v, ri=ri, b=bufs[ri], b2=bufs[ri + 1]: v.tensor_max(
                            b2[0:32, :], b[0:32, :], PSD[0:32, 512:1024]),
                            waits=((tag, "bfp", g, idx, ri),), inc="VE", key=(tag, "bfm", g, idx, ri))
                    we = [(tag, "bfm", g, idx, 2)]
                    if prev_bitm is not None:
                        we.append(prev_bitm)
                    P.em("vector", lambda v: v.tensor_tensor(
                        ONEHOT[0:32, :], LSB[0:32, :], LC[0:32, :], ALU.is_equal),
                        waits=we, inc="VE", key=(tag, "oh", g, idx))
                    for hl in range(HPQ):
                        w4 = [(tag, "oh", g, idx)]
                        if hl >= 2:
                            w4.append((tag, "bitcp", g, idx, hl - 2))
                        elif bits_extra:
                            w4 += list(bits_extra)
                        P.em("tensor", lambda t, hl=hl: t.matmul(
                            PSC[0:3, (hl % 2) * 512:(hl % 2) * 512 + 512],
                            CBb[0:96, 3 * hl:3 * hl + 3], ONEHOT[0:96, :],
                            start=True, stop=True),
                            waits=w4, inc="TE", key=(tag, "bitm", g, idx, hl))
                        P.em("vector", lambda v, hl=hl, dst=dst, c0=dstcol(hl): v.tensor_copy(
                            dst[64:67, c0:c0 + 512], PSC[0:3, (hl % 2) * 512:(hl % 2) * 512 + 512]),
                            waits=((tag, "bitm", g, idx, hl),), inc="VE",
                            key=(tag, "bitcp", g, idx, hl))

                def kb_lgm(jt):
                    for cc in range(8):
                        w = []
                        if cc == 0:
                            w = [("off", g, 1)] if jt == 0 else [("kb", "lg", g, jt - 1)]
                        P.em("tensor", lambda t, cc=cc, jt=jt, WRK=WRK: t.matmul(
                            PSD[0:32, 0:512], WRK[:, cc * 32:(cc + 1) * 32],
                            XNT[cc][:, jt * 512:(jt + 1) * 512],
                            start=(cc == 0), stop=(cc == 7)),
                            waits=w, inc="TE" if cc == 7 else None, key=("kb", "lgm", g, jt))

                def kb_lg(jt):
                    w = [("kb", "lgm", g, jt), "v_c0"]
                    LB_ = LSB if jt % 2 == 0 else LSB2
                    P.em("vector", lambda v, LB_=LB_: v.tensor_scalar(
                        LB_[0:32, :], PSD[0:32, 0:512], OFFK[:, :], None, ALU.add),
                        waits=w, inc="VE", key=("kb", "lg", g, jt))

                def kb_round(jt, ri):
                    LB_ = LSB if jt % 2 == 0 else LSB2
                    bufs = (LB_, LA, LB, LC)
                    P.em("tensor", lambda t, ri=ri, b=bufs[ri]: t.matmul(
                        PSD[0:96, 512:1024], PERMb[0:96, ri * 96:(ri + 1) * 96],
                        b[0:96, :], start=True, stop=True),
                        waits=(("kb", "lg", g, jt),) if ri == 0 else
                        (("kb", "bfm", g, jt, ri - 1),),
                        inc="TE", key=("kb", "bfp", g, jt, ri))
                    P.em("vector", lambda v, ri=ri, b=bufs[ri], b2=bufs[ri + 1]: v.tensor_max(
                        b2[0:32, :], b[0:32, :], PSD[0:32, 512:1024]),
                        waits=(("kb", "bfp", g, jt, ri),), inc="VE",
                        key=("kb", "bfm", g, jt, ri))

                def kb_tail(jt):
                    LB_ = LSB if jt % 2 == 0 else LSB2
                    we = [("kb", "bfm", g, jt, 2)]
                    if jt > 0:
                        we.append(("kb", "bitm", g, jt - 1, HPQ - 1))
                    P.em("vector", lambda v, LB_=LB_: v.tensor_tensor(
                        ONEHOT[0:32, :], LB_[0:32, :], LC[0:32, :], ALU.is_equal),
                        waits=we, inc="VE", key=("kb", "oh", g, jt))
                    for hl in range(HPQ):
                        w4 = [("kb", "oh", g, jt)]
                        if hl >= 2:
                            w4.append(("kb", "bitcp", g, jt, hl - 2))
                        elif jt > 0:
                            w4 += [("kb", "bitcp", g, jt - 1, 2), ("kb", "bitcp", g, jt - 1, 3)]
                        P.em("tensor", lambda t, hl=hl: t.matmul(
                            PSC[0:3, (hl % 2) * 512:(hl % 2) * 512 + 512],
                            CBb[0:96, 3 * hl:3 * hl + 3], ONEHOT[0:96, :],
                            start=True, stop=True),
                            waits=w4, inc="TE", key=("kb", "bitm", g, jt, hl))
                        P.em("vector", lambda v, hl=hl, jt=jt, AUGK=AUGK: v.tensor_copy(
                            AUGK[64:67, hl * SEQ + jt * 512:hl * SEQ + jt * 512 + 512],
                            PSC[0:3, (hl % 2) * 512:(hl % 2) * 512 + 512]),
                            waits=(("kb", "bitm", g, jt, hl),), inc="VE",
                            key=("kb", "bitcp", g, jt, hl))

                for jt in range(9):
                    if jt < 8:
                        kb_lgm(jt)
                    if jt >= 1:
                        for ri in range(3):
                            kb_round(jt - 1, ri)
                        kb_tail(jt - 1)
                    if jt < 8:
                        kb_lg(jt)

                for it in range(8):
                    for p in range(2):
                        qq = it * 2 + p
                        w = []
                        if qq == 0:
                            w = [("wqh", g), ("kev", g, 15, 1)]
                        if it > 0:
                            w.append(("exp", g, it - 1, 63 if p == 0 else 61))
                        for cc in range(8):
                            P.em("tensor", lambda t, cc=cc, p=p, it=it, qq=qq, WQH=WQH: t.matmul(
                                SCPS[qq % 2][:, 0:512],
                                WQH[:, cc * 256 + p * 128:cc * 256 + (p + 1) * 128],
                                XNT[cc][:, it * 512:(it + 1) * 512],
                                start=(cc == 0), stop=(cc == 7)),
                                waits=w if cc == 0 else (),
                                inc="TE" if cc == 7 else None, key=("qpm", g, qq))
                        for eo in range(2):
                            hl = 2 * p + eo
                            P.em("vector", lambda v, hl=hl, eo=eo, qq=qq, p=p, g=g, AUGQ=AUGQ:
                                 v.tensor_scalar(
                                     AUGQ[0:64, hl * 512:(hl + 1) * 512],
                                     SCPS[qq % 2][eo * 64:eo * 64 + 64, 0:512],
                                     BQTF[eo * 64:eo * 64 + 64, 2 * g + p:2 * g + p + 1],
                                     None, ALU.add),
                                 waits=(("qpm", g, qq),), inc="VE", key=("qev", g, qq, eo))
                    if it == 0:
                        lex = [("kb", "lg", g, 7)]
                        pbm = ("kb", "bitm", g, 7, HPQ - 1)
                        bex = [("kb", "bitcp", g, 7, 2), ("kb", "bitcp", g, 7, 3)]
                    else:
                        lex = [("rc", g, it - 1, HPQ - 1), ("exp", g, it - 1, 62)]
                        pbm = ("qb", "bitm", g, it - 1, HPQ - 1)
                        bex = [("cmul", g, it - 1, HPQ - 1)]
                    bits_block("qb", it,
                               lambda cc, it=it: XNT[cc][:, it * 512:(it + 1) * 512],
                               WRQ, OFFQ, AUGQ, lambda hl: hl * 512,
                               logit_extra=lex, prev_bitm=pbm, bits_extra=bex)
                    LAG = 3
                    SLOT = (PSA, PSB, PSD)

                    def emit_scores(lin, it=it, g=g):
                        hl, jj = divmod(lin, 16)
                        w5 = []
                        if lin == 0:
                            w5 = [("qev", g, it * 2 + 1, 1),
                                  ("qb", "bitcp", g, it, HPQ - 1),
                                  ("kb", "bitcp", g, 7, HPQ - 1),
                                  ("kev", g, 15, 1), ("vev", g, 31)]
                        if lin >= LAG:
                            w5.append(("exp", g, it, lin - LAG))
                        elif it > 0:
                            w5.append(("exp", g, it - 1, 61 + lin))
                        if lin == 2:
                            w5.append(("qb", "bfm", g, it, 2))
                        for sub in range(2):
                            j = 2 * jj + sub
                            P.em("tensor", lambda t, hl=hl, j=j, lin=lin, sub=sub,
                                 AUGK=AUGK, AUGQ=AUGQ: t.matmul(
                                     SLOT[lin % 3][:, sub * 512:sub * 512 + 512],
                                     AUGK[0:96, hl * SEQ + j * 128: hl * SEQ + (j + 1) * 128],
                                     AUGQ[0:96, hl * 512:(hl + 1) * 512],
                                     start=True, stop=True),
                                 waits=w5 if sub == 0 else (),
                                 inc="TE" if sub == 1 else None, key=("scm", g, it, lin))
                        we2 = [("scm", g, it, lin)]
                        if lin >= 10:
                            we2.append(("pvm", g, it, lin - 10))
                        elif it > 0:
                            we2.append(("pvm", g, it - 1, 54 + lin))
                        P.em("scalar", lambda s, lin=lin: s.activation(
                            EXPS[lin % 10][:, :], SLOT[lin % 3][:, :], AF.Exp, bias=EXP_BIAS),
                            waits=we2, inc="SE", key=("exp", g, it, lin))

                    def emit_pv(plin, it=it, g=g):
                        hl, jjp = divmod(plin, 16)
                        for sub in range(2):
                            j = 2 * jjp + sub
                            w6 = [("exp", g, it, plin)] if sub == 0 else []
                            if jjp == 0 and sub == 0 and (it > 0 or hl >= 2):
                                pit, phl = (it, hl - 2) if hl >= 2 else (it - 1, hl + 2)
                                w6.append(("cmul", g, pit, phl))
                            P.em("tensor", lambda t, hl=hl, j=j, plin=plin, sub=sub,
                                 VAUG=VAUG: t.matmul(
                                     PSC[0:65, (hl % 2) * 512:(hl % 2) * 512 + 512],
                                     VAUG[:, (j * HPQ + hl) * 65:(j * HPQ + hl) * 65 + 65],
                                     EXPS[plin % 10][:, sub * 512:sub * 512 + 512],
                                     start=(jjp == 0 and sub == 0), stop=(jjp == 15 and sub == 1)),
                                 waits=w6, inc="TE" if sub == 1 else None,
                                 key=("pvm", g, it, plin))
                        if jjp == 15:
                            emit_tail(hl)

                    def emit_tail(hl, it=it, g=g):
                        par = hl % 2
                        wdn = [("pvm", g, it, hl * 16 + 15)]
                        if debug and (g, it, hl) == (0, 7, 1):
                            wdn.append("dbg_dn_st")
                        P.em("vector", lambda v, hl=hl, par=par: v.tensor_copy(
                            DNBF[par][0:1, :], PSC[64:65, (hl % 2) * 512:(hl % 2) * 512 + 512]),
                            waits=wdn, inc="VE", key=("dn", g, it, hl))
                        if debug and (g, it, hl) == (0, 7, 0):
                            P.em("sync", lambda sy: sy.dma_start(d_ddn, DNBF[0][:, :]),
                                 waits=(("dn", 0, 7, 0),), inc="Cdbg", inck=16, key="dbg_dn_st")
                            P.em("vector", lambda v: v.tensor_copy(RCF[0][:, :], PSC[0:64, 0:512]),
                                 waits=(("pvm", 0, 7, 15),), inc="VE", key="dbg_pv_ev")
                            P.em("sync", lambda sy: sy.dma_start(d_dpv, RCF[0][:, :]),
                                 waits=("dbg_pv_ev",), inc="Cdbg", inck=16, key="dbg_pv_st")
                        w7 = [("dn", g, it, hl)]
                        P.em("tensor", lambda t, hl=hl, par=par: t.matmul(
                            PSC[64:128, (hl % 2) * 512:(hl % 2) * 512 + 512],
                            ONESB[0:1, 0:64], DNBF[par][0:1, :],
                            start=True, stop=True),
                            waits=w7, inc="TE", key=("bcm", g, it, hl))
                        wbe = [("bcm", g, it, hl)]
                        if it * 4 + hl >= 2:
                            k3 = it * 4 + hl - 2
                            wbe.append(("rc", g, k3 // 4, k3 % 4))
                        P.em("vector", lambda v, hl=hl, par=par: v.tensor_copy(
                            BCS[par][:, :], PSC[64:128, (hl % 2) * 512:(hl % 2) * 512 + 512]),
                            waits=wbe, inc="VE", key=("bce", g, it, hl))

                    def emit_tail2(hl, it=it, g=g):
                        par = hl % 2
                        HG = 4 * g + hl
                        pr = (hl // 2) % 2
                        wrl = [("bce", g, it, hl)]
                        if debug and (g, it, hl) == (0, 7, 0):
                            wrl.append("dbg_pv_st")
                        P.em("scalar", lambda s, par=par: s.activation(
                            RCF[par][:, :], BCS[par][:, :], AF.Ln),
                            waits=wrl, inc="SE", key=("rcl", g, it, hl))
                        P.em("scalar", lambda s, par=par: s.activation(
                            BCS[par][:, :], RCF[par][:, :], AF.Exp, scale=np.float32(-1.0)),
                            waits=(("rcl", g, it, hl),), inc="SE", key=("rc", g, it, hl))
                        if debug and (g, it, hl) == (0, 7, 0):
                            P.em("sync", lambda sy: sy.dma_start(d_dbcs, BCS[0][:, :]),
                                 waits=(("rc", 0, 7, 0),), inc="Cdbg", inck=16, key="dbg_bcs_st")
                        w8 = [("rc", g, it, hl)]
                        if it * 4 + hl >= 4:
                            k2 = it * 4 + hl - 4
                            w8.append(("cst", g, k2 // 4, (k2 % 4) // 2))
                        P.em("vector", lambda v, hl=hl, pr=pr, par=par: v.tensor_tensor(
                            CATS[pr][(hl % 2) * 64:(hl % 2) * 64 + 64, :],
                            PSC[0:64, (hl % 2) * 512:(hl % 2) * 512 + 512],
                            BCS[par][0:64, :], ALU.mult),
                            waits=w8, inc="VE", key=("cmul", g, it, hl))
                        if hl % 2 == 1:
                            ch = HG // 2
                            P.em("sync", lambda sy, ch=ch, it=it, pr=pr: sy.dma_start(
                                d_cat[ch * 128:(ch + 1) * 128, it * 512:(it + 1) * 512],
                                CATS[pr][:, :]),
                                waits=(("cmul", g, it, hl),), inc=f"Ccst{pr}", inck=16,
                                key=("cst", g, it, hl // 2))

                    pend = []
                    for lin in range(64 + LAG + 4):
                        if lin < 64:
                            emit_scores(lin)
                        if LAG <= lin < 64 + LAG:
                            emit_pv(lin - LAG)
                        if pend and pend[0][0] <= lin:
                            emit_tail2(pend.pop(0)[1])
                        if lin >= LAG and (lin - LAG) % 16 == 15:
                            pend.append((lin + 4, (lin - LAG) // 16))
                    while pend:
                        emit_tail2(pend.pop(0)[1])
            if debug and g == 0:
                P.em("sync", lambda sy, AUGK=AUGK: sy.dma_start(d_daugk, AUGK[:, :]),
                     inc="ST", inck=16)
                P.em("sync", lambda sy, VAUG=VAUG: sy.dma_start(d_dvaug, VAUG[:, :]),
                     inc="ST", inck=16)
                P.em("sync", lambda sy, AUGQ=AUGQ: sy.dma_start(d_daugq, AUGQ[:, :]),
                     inc="ST", inck=16)
            P.barrier()
        xnt_scope.close()

        pre = ExitStack()
        with pre:
            WGI = [pre.enter_context(nc.sbuf_tensor(f"wgi{i}", [128, 1024], BF16)) for i in range(4)]
            TST = [pre.enter_context(nc.sbuf_tensor(f"tst{i}", [128, 2048], BF16)) for i in range(2)]
            WDI = [pre.enter_context(nc.sbuf_tensor(f"wdi{i}", [128, 3072], BF16)) for i in range(4)]
            TSD = [pre.enter_context(nc.sbuf_tensor(f"tsd{i}", [128, 128], BF16)) for i in range(6)]

            TSL = [(PSC, 0), (PSA, 0), (PSB, 0), (PSD, 0),
                   (PSC, 512), (PSA, 512), (PSB, 512), (PSD, 512)]
            tser = [0]
            kkd = [0]

            def wgu_unit(li):
                hc, half = divmod(li, 2)
                wsrc = d_wg if half == 0 else d_wu
                w = [("pgt", li - 4, 7)] if li >= 4 else []
                P.em("gpsimd", lambda gp, hc=hc, li=li, wsrc=wsrc: gp.dma_start(
                    WGI[li % 4][:, :], wsrc[hc * 128:(hc + 1) * 128, :]),
                    waits=w, inc=f"Ggi{li % 4}", inck=16, key=("pgl", li))
                for cb in range(8):
                    ts = tser[0]
                    pt, po = TSL[ts % 8]
                    w2 = [("pgl", li)] if cb == 0 else []
                    if li == 0 and cb == 0:
                        w2.append("ld_idb")
                    if ts >= 8:
                        w2.append(("tev", ts - 8))
                    P.em("tensor", lambda t, li=li, cb=cb, pt=pt, po=po: t.matmul(
                        pt[:, po:po + 128],
                        WGI[li % 4][:, cb * 128:(cb + 1) * 128], IDENTB[:, :],
                        start=True, stop=True),
                        waits=w2, inc="TE", key=("pgt", li, cb))
                    w3 = [("pgt", li, cb)]
                    if cb == 0 and half == 0 and hc >= 2:
                        w3.append(("pgs", hc - 2))
                    P.em("vector", lambda v, half=half, cb=cb, li=li, pt=pt, po=po: v.tensor_copy(
                        TST[(li // 2) % 2][:, half * 1024 + cb * 128: half * 1024 + (cb + 1) * 128],
                        pt[:, po:po + 128]),
                        waits=w3, inc="VE", key=("tev", ts))
                    tser[0] += 1
                if half == 1:
                    P.em("sync", lambda sy, hc=hc, ts=tser[0] - 1: sy.dma_start(
                        d_wgu[hc * 128:(hc + 1) * 128, :], TST[hc % 2][:, :]),
                        waits=(("tev", ts),), inc=f"Ctst{hc % 2}", inck=16, key=("pgs", hc))

            def wd_unit(rb):
                w = [("pdt", rb - 4, 23)] if rb >= 4 else []
                P.em("gpsimd", lambda gp, rb=rb: gp.dma_start(
                    WDI[rb % 4][:, :], d_wd[rb * 128:(rb + 1) * 128, :]),
                    waits=w, inc=f"Gdi{rb % 4}", inck=16, key=("pdl", rb))
                for hb in range(24):
                    ts = tser[0]
                    kk = kkd[0]
                    pt, po = TSL[ts % 8]
                    w2 = [("pdl", rb)] if hb == 0 else []
                    if ts >= 8:
                        w2.append(("tev", ts - 8))
                    P.em("tensor", lambda t, rb=rb, hb=hb, pt=pt, po=po: t.matmul(
                        pt[:, po:po + 128],
                        WDI[rb % 4][:, hb * 128:(hb + 1) * 128], IDENTB[:, :],
                        start=True, stop=True),
                        waits=w2, inc="TE", key=("pdt", rb, hb))
                    w3 = [("pdt", rb, hb)]
                    if kk >= 6:
                        w3.append(("pds", kk - 6))
                    P.em("vector", lambda v, hb=hb, kk=kk, pt=pt, po=po: v.tensor_copy(
                        TSD[kk % 6][:, :], pt[:, po:po + 128]),
                        waits=w3, inc="VE", key=("tev", ts))
                    P.em("sync", lambda sy, rb=rb, hb=hb, kk=kk: sy.dma_start(
                        d_wdt[hb * 128:(hb + 1) * 128, rb * 128:(rb + 1) * 128],
                        TSD[kk % 6][:, :]),
                        waits=(("tev", ts),), inc=f"Ctsd{kk % 6}", inck=16, key=("pds", kk))
                    tser[0] += 1
                    kkd[0] += 1

            for grp in range(8):
                for li in range(grp * 6, grp * 6 + 6):
                    wgu_unit(li)
                wd_unit(grp)
        P.barrier()

        ds = ExitStack()
        with ds:
            DT = ds.enter_context
            XB = [DT(nc.sbuf_tensor("xb0", [128, 4096], F32))]
            WGUC = [DT(nc.sbuf_tensor(f"wguc{i}", [128, 2048], BF16)) for i in range(4)]
            GSB = [DT(nc.sbuf_tensor(f"gsb{i}", [128, 512], F32)) for i in range(2)]
            HTS = DT(nc.sbuf_tensor("hts", [128, 24 * 512], BF16))
            WDTS = DT(nc.sbuf_tensor("wdts", [128, 24 * 1024], BF16))
            WOSB = DT(nc.sbuf_tensor("wosb", [128, 8 * 1024], BF16))
            CTC = [DT(nc.sbuf_tensor("ctc0", [128, 4096], BF16))]
            OSB = [DT(nc.sbuf_tensor("osb0", [128, 1024], F32))]
            DSTAT = DT(nc.sbuf_tensor("dstat_s", [128, 256], F32)) if debug else None

            for q in range(4):
                P.em("sync", lambda sy, q=q: sy.dma_start(
                    WDTS[:, q * 6144:(q + 1) * 6144].rearrange("p (hc e) -> p hc e", e=1024),
                    d_wdt[q * 768 * 1024 // 1024 * 1024 // 1024 * 0 + q * 6 * 128:(q + 1) * 6 * 128, :]
                    .rearrange("(hc p) e -> p hc e", p=128)),
                    inc=f"Cwdt{q}", inck=16, key=("wdl", q))
            P.em("gpsimd", lambda gp: gp.dma_start(
                WOSB[:, :].rearrange("p (ch e) -> p ch e", e=1024),
                d_w["wo"].rearrange("(ch p) e -> p ch e", p=128)),
                inc="Gwo", inck=16, key="wol")

            for ib in range(8):
                P.em("sync", lambda sy, ib=ib: sy.dma_start(
                    CTC[0][:, :].rearrange("p (ch i) -> p ch i", i=512),
                    d_cat[:, ib * 512:(ib + 1) * 512].rearrange("(ch p) i -> p ch i", p=128)),
                    waits=(("fev", ib - 1, 3),) if ib >= 1 else (), inc="Cctc0", inck=16,
                    key=("ctl", ib))
                w = [("fev", ib - 1, s_) for s_ in range(4)] if ib >= 1 else []
                P.em("sync", lambda sy, ib=ib: sy.dma_start(
                    XB[0][:, :].rearrange("p (st d) -> p st d", d=1024),
                    d_x[ib * 512:(ib + 1) * 512, :].rearrange("(st p) d -> p st d", p=128)),
                    waits=w, inc="Cxd0", inck=16, key=("D", "xload", ib))
                for hc in range(24):
                    lk = ib * 24 + hc
                    w = [("gum", lk - 4)] if lk >= 4 else []
                    P.em("gpsimd", lambda gp, hc=hc, lk=lk: gp.dma_start(
                        WGUC[lk % 4][:, :], d_wgu[hc * 128:(hc + 1) * 128, :]),
                        waits=w, inc=f"Ggu{lk % 4}", inck=16, key=("gul", lk))
                    wg_ = [("gul", lk)]
                    if lk >= 2:
                        wg_.append(("hts", lk - 2))
                    for cc in range(8):
                        P.em("tensor", lambda t, hc=hc, cc=cc, lk=lk, ib=ib: t.matmul(
                            SCPS[hc % 2][:, 0:512],
                            WGUC[lk % 4][:, cc * 128:(cc + 1) * 128],
                            XNT[cc][:, ib * 512:(ib + 1) * 512],
                            start=(cc == 0), stop=(cc == 7)),
                            waits=wg_ if cc == 0 else ())
                    for cc in range(8):
                        P.em("tensor", lambda t, hc=hc, cc=cc, lk=lk, ib=ib: t.matmul(
                            SCPS[hc % 2][:, 512:1024],
                            WGUC[lk % 4][:, 1024 + cc * 128:1024 + (cc + 1) * 128],
                            XNT[cc][:, ib * 512:(ib + 1) * 512],
                            start=(cc == 0), stop=(cc == 7)),
                            inc="TE" if cc == 7 else None, key=("gum", lk))
                    ws = [("gum", lk)]
                    if lk >= 2:
                        ws.append(("hts", lk - 2))
                    P.em("scalar", lambda s, hc=hc, lk=lk: s.activation(
                        GSB[hc % 2][:, :], SCPS[hc % 2][:, 0:512], AF.Silu, scale=C127),
                        waits=ws, inc="SE", key=("silu", lk))
                    wh = [("silu", lk)]
                    if ib > 0:
                        wh.append(("fin", ib * 4 - 1))
                    if DBG_XNTB[0] and lk == 24:
                        wh.append("dbg_hts_st")
                    P.em("vector", lambda v, hc=hc, lk=lk: v.scalar_tensor_tensor(
                        HTS[:, hc * 512:(hc + 1) * 512], SCPS[hc % 2][:, 512:1024],
                        C127SQ, GSB[hc % 2][:, :], ALU.mult, ALU.mult),
                        waits=wh, inc="VE", key=("hts", lk))
                if debug and ib == 0:
                    P.em("sync", lambda sy: sy.dma_start(d_dhts, HTS[:, :]),
                         waits=(("hts", 23),), inc="Cdbg", inck=16, key="dbg_hts_st")
                for st in range(4):
                    tt = ib * 4 + st
                    w = [("hts", ib * 24 + 23), ("ctl", ib), "wol"]
                    if tt == 0:
                        w += [("wdl", q_) for q_ in range(4)]
                    if tt >= 1:
                        w.append(("fev", (tt - 1) // 4, (tt - 1) % 4))
                    for eh in range(2):
                        P.em("tensor", lambda t, eh=eh: t.matmul(
                            PSD[:, eh * 512:(eh + 1) * 512], ONESB[0:1, 0:128],
                            BOROW[0:1, eh * 512:(eh + 1) * 512], start=True, stop=False),
                            waits=w if eh == 0 else ())
                    for ch in range(8):
                        for eh in range(2):
                            P.em("tensor", lambda t, ch=ch, eh=eh, ib=ib, st=st: t.matmul(
                                PSD[:, eh * 512:(eh + 1) * 512],
                                CTC[0][:, ch * 512 + st * 128: ch * 512 + (st + 1) * 128],
                                WOSB[:, ch * 1024 + eh * 512: ch * 1024 + (eh + 1) * 512],
                                start=False, stop=False))
                    for hc in range(24):
                        for eh in range(2):
                            last = (hc == 23)
                            P.em("tensor", lambda t, hc=hc, eh=eh, st=st, last=last: t.matmul(
                                PSD[:, eh * 512:(eh + 1) * 512],
                                HTS[:, hc * 512 + st * 128: hc * 512 + (st + 1) * 128],
                                WDTS[:, hc * 1024 + eh * 512: hc * 1024 + (eh + 1) * 512],
                                start=False, stop=last),
                                inc="TE" if (last and eh == 1) else None, key=("fin", tt))
                    wf = [("fin", tt), ("D", "xload", ib)]
                    if tt >= 1:
                        wf.append(("ost", tt - 1))
                    P.em("vector", lambda v, ib=ib, st=st, tt=tt: v.tensor_tensor(
                        OSB[0][:, :], PSD[:, :], XB[0][:, st * 1024:(st + 1) * 1024],
                        ALU.add),
                        waits=wf, inc="VE", key=("fev", ib, st))
                    P.em("sync", lambda sy, tt=tt: sy.dma_start(
                        d_out[tt * 128:(tt + 1) * 128, :], OSB[0][:, :]),
                        waits=(("fev", tt // 4, tt % 4),), inc="Cosb0", inck=16, key=("ost", tt))

            if debug:
                P.em("sync", lambda sy: sy.dma_start(d_dstat, SREG[:, 0:64].rearrange(
                    "p c -> p c")), waits=(("fev", 7, 3),), inc="Cdbg", inck=16)
            fin_waits = tuple(("LIT", s, P.cnt[s]) for s in list(P.cnt)
                              if s.startswith(("Cosb", "Ccst", "Ctst", "Ctsd")))
            P.em("sync", lambda sy: None, waits=fin_waits)

        P.emit(block)
    return nc


_NC_CACHE = None


def kernel(**inputs):
    global _NC_CACHE
    ident, cb, perms = _host_consts()
    x = np.ascontiguousarray(np.asarray(inputs["L0_attn_x"], np.float32))
    base = {
        "wq": np.ascontiguousarray(np.asarray(inputs["L0_attn_Wq"], np.float32)),
        "wk": np.ascontiguousarray(np.asarray(inputs["L0_attn_Wk"], np.float32)),
        "wv": np.ascontiguousarray(np.asarray(inputs["L0_attn_Wv"], np.float32)),
        "wo": np.ascontiguousarray(np.asarray(inputs["L0_attn_Wo"], np.float32)),
        "bq": np.ascontiguousarray(np.asarray(inputs["L0_attn_bq"], np.float32)),
        "bk": np.ascontiguousarray(np.asarray(inputs["L0_attn_bk"], np.float32)),
        "bv": np.ascontiguousarray(np.asarray(inputs["L0_attn_bv"], np.float32)),
        "bo": np.ascontiguousarray(np.asarray(inputs["L0_attn_bo"], np.float32)),
        "r": np.ascontiguousarray(np.asarray(inputs["L0_attn_R"], np.float32)),
        "wg": np.ascontiguousarray(np.asarray(inputs["L0_ffn_w_gate"], np.int32)),
        "wu": np.ascontiguousarray(np.asarray(inputs["L0_ffn_w_up"], np.int32)),
        "wd": np.ascontiguousarray(np.asarray(inputs["L0_ffn_w_down"], np.int32)),
        "identf": ident, "cbc": cb, "permsc": perms,
    }
    if _NC_CACHE is None:
        _NC_CACHE = build_program()
    nc = _NC_CACHE
    in_maps = [dict(base, x=np.ascontiguousarray(x[b])) for b in range(8)]
    res = run_bass_kernel_spmd(nc, in_maps, core_ids=list(range(8)))
    out = np.stack([np.asarray(res.results[b]["out"], np.float32) for b in range(8)], axis=0)
    return out


if __name__ == "__main__":
    nc = build_program()
    print("built ok; instructions:", nc.n_instructions())
